# version 23
# baseline (speedup 1.0000x reference)
import numpy as np
import contextlib
import concourse.bass as bass
import concourse.mybir as mybir
from concourse.bass_utils import run_bass_kernel_spmd

F32 = mybir.dt.float32
BF16 = mybir.dt.bfloat16
AF = mybir.ActivationFunctionType
ALU = mybir.AluOpType
AX = mybir.AxisListType

D = 1024
H = 8
DFF = 4096
SCALE = float(1.0 / np.sqrt(160.0))
LN_EPS = 1e-5
RMS_EPS = 1e-8
N_CORES = 8


class AliasTracker(list):

    def __init__(self):
        super().__init__([None])

    def __getitem__(self, i):
        return super().__getitem__(0)

    def __setitem__(self, i, v):
        super().__setitem__(0, v)


class Builder:
    ENGS = ("tensor", "vector", "scalar", "sync", "gpsimd")
    E2S = {"tensor": "t", "vector": "v", "scalar": "s", "sync": "d", "gpsimd": "g"}

    DK = {"sync": 8, "gpsimd": 4, "scalar": 4}

    def __init__(self, nc, stack):
        self.nc = nc
        self.q = {e: [] for e in self.ENGS}
        keys = (["t", "v", "s"] + [f"d{i}" for i in range(8)]
                + [f"g{i}" for i in range(4)] + [f"e{i}" for i in range(4)])
        self.c = {k: 0 for k in keys}
        self.sem = {k: stack.enter_context(nc.semaphore("sem_" + k)) for k in keys}
        self.hw = {e: {k: 0 for k in keys} for e in self.ENGS}
        self.pending = {e: [] for e in self.ENGS}
        self.dptr = {"sync": 0, "gpsimd": 0, "scalar": 0}

    def barrier(self):
        snap = [(k, self.c[k]) for k in self.c]
        for e in self.ENGS:
            self.pending[e] = list(snap)

    def _emit(self, eng, fn, waits, semkey=None):
        hw = self.hw[eng]
        wl = []
        allw = self.pending[eng] + [w for w in waits if w is not None]
        self.pending[eng] = []
        for k, val in allw:
            assert val <= self.c[k], f"wait on future {k}:{val} > {self.c[k]}"
            if val > hw[k]:
                hw[k] = val
                wl.append((self.sem[k], val))
        sk = semkey if semkey is not None else self.E2S[eng]
        inc = 16 if (len(sk) > 1 and sk[0] in ("d", "g", "e")) else 1
        self.c[sk] += inc
        tok = self.c[sk]
        sem = self.sem[sk]

        def run(e, fn=fn, wl=wl, sem=sem, inc=inc):
            for (h, v) in wl:
                e.wait_ge(h, v)
            fn(e).then_inc(sem, inc)

        self.q[eng].append(run)
        return (sk, tok)

    def t(self, fn, *waits):
        return self._emit("tensor", fn, waits)

    def v(self, fn, *waits):
        return self._emit("vector", fn, waits)

    def s(self, fn, *waits):
        return self._emit("scalar", fn, waits)

    def dma(self, eng, out, in_, *waits, transpose=False, slow=False):
        def fn(e, out=out, in_=in_, transpose=transpose, slow=slow):
            kw = {}
            if transpose:
                kw["transpose"] = True
            if slow:
                kw["allow_slow_non_contiguous"] = True
            return e.dma_start(out=out, in_=in_, **kw)
        K = self.DK[eng]
        ptr = self.dptr[eng]
        self.dptr[eng] = (ptr + 1) % K
        key = {"sync": "d", "gpsimd": "g", "scalar": "e"}[eng] + str(ptr)
        chain = (key, self.c[key])
        return self._emit(eng, fn, list(waits) + [chain], semkey=key)

    def fence(self, eng):
        pre = "d" if eng == "sync" else "g"
        K = self.DK[eng]
        return [(pre + str(i), self.c[pre + str(i)]) for i in range(K)]

    def raw(self, eng, fn):
        self.q[eng].append(fn)


def build_nc(S):
    NT = S // 128
    NQ = S // 512
    NK = S // 128
    TB = S // 256
    NG = -(-NK // 3)
    nc = bass.Bass("TRN2", target_bir_lowering=False)

    x_h = nc.dram_tensor("x", [S, D], F32, kind="ExternalInput").ap()
    out_h = nc.dram_tensor("out", [S, D], F32, kind="ExternalOutput").ap()
    wdq_h = nc.dram_tensor("W_DQ", [D, 32], F32, kind="ExternalInput").ap()
    bdq_h = nc.dram_tensor("b_DQ", [32], F32, kind="ExternalInput").ap()
    wuq_h = nc.dram_tensor("W_UQ", [32, D], F32, kind="ExternalInput").ap()
    buq_h = nc.dram_tensor("b_UQ", [D], F32, kind="ExternalInput").ap()
    wdkv_h = nc.dram_tensor("W_DKV", [D, 32], F32, kind="ExternalInput").ap()
    bdkv_h = nc.dram_tensor("b_DKV", [32], F32, kind="ExternalInput").ap()
    wuk_h = nc.dram_tensor("W_UK", [32, D], F32, kind="ExternalInput").ap()
    wuv_h = nc.dram_tensor("W_UV", [32, D], F32, kind="ExternalInput").ap()
    buv_h = nc.dram_tensor("b_UV", [D], F32, kind="ExternalInput").ap()
    wqr_h = nc.dram_tensor("W_QR", [32, 256], F32, kind="ExternalInput").ap()
    bqr_h = nc.dram_tensor("b_QR", [256], F32, kind="ExternalInput").ap()
    wkr_h = nc.dram_tensor("W_KR", [D, 32], F32, kind="ExternalInput").ap()
    bkr_h = nc.dram_tensor("b_KR", [32], F32, kind="ExternalInput").ap()
    wo_h = nc.dram_tensor("W_O", [D, D], F32, kind="ExternalInput").ap()
    bo_h = nc.dram_tensor("b_O", [D], F32, kind="ExternalInput").ap()
    wup_h = nc.dram_tensor("w_up", [DFF, D], F32, kind="ExternalInput").ap()
    wdn_h = nc.dram_tensor("w_down", [D, DFF], F32, kind="ExternalInput").ap()
    ln1s = nc.dram_tensor("ln1s", [S, D], BF16, kind="Internal").ap()
    xns = nc.dram_tensor("xns", [S, D], BF16, kind="Internal").ap()
    x2s = nc.dram_tensor("x2s", [S, D], F32, kind="Internal").ap()
    wups = nc.dram_tensor("wups", [DFF, D], BF16, kind="Internal").ap()
    wdns = nc.dram_tensor("wdns", [D, DFF], BF16, kind="Internal").ap()

    stack = contextlib.ExitStack()
    b = Builder(nc, stack)
    sb = lambda n, shp, dt=BF16: stack.enter_context(nc.sbuf_tensor(n, shp, dt))

    ones_row = sb("ones_row", [1, 128], BF16)
    ones_col = sb("ones_col", [128, 1], BF16)
    onesf_row = sb("onesf_row", [1, 128], F32)
    accw = sb("accw", [128, 1], F32)
    bkh = sb("bkh", [64, 1], F32)
    bdq = sb("bdq", [32, 1], F32)
    bias64 = sb("bias64", [64, 8], F32)
    beff = sb("beff", [1, D], BF16)
    wc_pack = [sb(f"wc{j}", [128, D], BF16) for j in range(2)]
    lwabs = sb("lwabs", [128, 512], BF16)
    cqT = sb("cqT", [128, S], BF16)
    khT = sb("khT", [128, S], BF16)
    ckv1 = sb("ckv1", [128, 128 * NK], BF16)
    tau_p = sb("tau_p", [128, 2], F32)
    tau_n = sb("tau_n", [128, 2], F32)
    acc = sb("acc", [128, 2], F32)
    x_sb = [sb(f"x_sb{i}", [128, D], F32) for i in range(2)]
    pool = [sb(f"pool{i}", [128, 16], F32) for i in range(2)]
    stt = [sb(f"stt{i}", [128, 12], F32) for i in range(2)]

    w_in = [sb(f"w_in{i}", [128, D], F32) for i in range(2)]
    wt_bf = [sb(f"wt_bf{i}", [128, D], BF16) for i in range(2)]
    wt_tmp = sb("wt_tmp", [128, D], BF16)
    wred = sb("wred", [128, 1], F32)
    tau_sb = sb("tau_sb", [1, 2], F32)

    def chunk_src(i):
        if i < 32:
            return wup_h[i * 128:(i + 1) * 128, :]
        j = i - 32
        return wdn_h[(j // 4) * 128:(j // 4 + 1) * 128, (j % 4) * D:(j % 4 + 1) * D]

    scA = contextlib.ExitStack()
    sbA = lambda n, shp, dt=BF16: scA.enter_context(nc.sbuf_tensor(n, shp, dt))
    lnT = [sbA(f"lnT{d}", [128, S], BF16) for d in range(8)]
    wo_st = [sbA(f"wo_st{i}", [128, D], F32) for i in range(2)]
    wo_bf = sbA("wo_bf", [128, 8 * D], BF16)
    wkh_f = sbA("wkh_f", [128, 8 * 64], F32)
    wkh = sbA("wkh", [128, 8 * 64], BF16)
    wdq_f = sbA("wdq_f", [128, 8 * 32], F32)
    wdqb = sbA("wdqb", [128, 8 * 32], BF16)
    wuq_f = sbA("wuq_f", [32, D], F32)
    wuqb = sbA("wuqb", [32, D], BF16)
    wuk_f = sbA("wuk_f", [32, D], F32)
    wukb = sbA("wukb", [32, D], BF16)
    wuv_f = sbA("wuv_f", [32, D], F32)
    wuvb = sbA("wuvb", [32, D], BF16)
    wqr_f = sbA("wqr_f", [32, 256], F32)
    wqrb = sbA("wqrb", [32, 256], BF16)
    wuqT = sbA("wuqT", [128, 8 * 32], BF16)
    wukT = sbA("wukT", [128, 8 * 32], BF16)
    wuvT = sbA("wuvT", [128, 8 * 32], BF16)
    buq_f = sbA("buq_f", [128, 8], F32)
    buqb = sbA("buqb", [128, 8], BF16)
    buv_f = sbA("buv_f", [128, 8], F32)
    buvb = sbA("buvb", [128, 8], BF16)
    bo_sb = sbA("bo_sb", [1, D], F32)
    beff_f = sbA("beff_f", [1, D], F32)
    ln1b = [sbA(f"ln1b{i}", [128, D], BF16) for i in range(2)]
    wc_st = [sbA(f"wc_st{i}", [32, D], BF16) for i in range(2)]

    psA = contextlib.ExitStack()
    pkh = [psA.enter_context(nc.psum_tensor(f"pkh{i}", [64, 512], F32)) for i in range(2)]
    pcq = [psA.enter_context(nc.psum_tensor(f"pcq{i}", [32, 512], F32)) for i in range(2)]
    pmA = psA.enter_context(nc.psum_tensor("pmA", [128, 512], F32))
    pmB = psA.enter_context(nc.psum_tensor("pmB", [128, 512], F32))

    t_ones_row = b.v(lambda e: e.memset(ones_row[:], 1.0))
    t_ones_col = b.v(lambda e: e.memset(ones_col[:], 1.0))
    t_onesf_row = b.v(lambda e: e.memset(onesf_row[:], 1.0))
    t_accw = b.v(lambda e: e.memset(accw[:], 1.0))
    b.v(lambda e: e.memset(acc[:], 0.0))
    t_ckv1_i = b.v(lambda e: e.memset(ckv1[:], 0.0))
    _cap = ckv1[:, 32:33]
    t_ckv1_o = b.v(lambda e: e.memset(
        bass.AP(_cap.tensor, _cap.offset, [[128 * NK, 128], [128, NK]]), 1.0), t_ckv1_i)
    t_khT_z = b.v(lambda e: e.memset(khT[64:128, :], 0.0))
    t_lw_za = b.v(lambda e: e.memset(lwabs[32:64, :], 0.0))
    t_lw_z = b.v(lambda e: e.memset(lwabs[64:128, :], 0.0), t_lw_za)
    t_cq_za = b.v(lambda e: e.memset(cqT[32:64, :], 0.0))
    t_cq_z = b.v(lambda e: e.memset(cqT[64:128, :], 0.0), t_cq_za)

    d_wkh = []
    for dt in range(8):
        d_wkh.append(b.dma("sync", wkh_f[:, dt * 64:dt * 64 + 32],
                           wdkv_h[dt * 128:(dt + 1) * 128, :]))
        d_wkh.append(b.dma("sync", wkh_f[:, dt * 64 + 32:dt * 64 + 64],
                           wkr_h[dt * 128:(dt + 1) * 128, :]))
    t_wkh = b.v(lambda e: e.tensor_copy(wkh[:], wkh_f[:]), *d_wkh)
    d_wdq = [b.dma("sync", wdq_f[:, dt * 32:(dt + 1) * 32],
                   wdq_h[dt * 128:(dt + 1) * 128, :]) for dt in range(8)]
    t_wdq = b.v(lambda e: e.tensor_copy(wdqb[:], wdq_f[:]), *d_wdq)
    d_uq = b.dma("sync", wuq_f[:], wuq_h[:])
    t_uq = b.v(lambda e: e.tensor_copy(wuqb[:], wuq_f[:]), d_uq)
    d_uk = b.dma("sync", wuk_f[:], wuk_h[:])
    t_uk = b.v(lambda e: e.tensor_copy(wukb[:], wuk_f[:]), d_uk)
    d_uv = b.dma("sync", wuv_f[:], wuv_h[:])
    t_uv = b.v(lambda e: e.tensor_copy(wuvb[:], wuv_f[:]), d_uv)
    d_qr = b.dma("sync", wqr_f[:], wqr_h[:])
    t_qr = b.v(lambda e: e.tensor_copy(wqrb[:], wqr_f[:]), d_qr)
    d_bdq = b.dma("sync", bdq[:], bdq_h.rearrange("(p o) -> p o", o=1), slow=True)
    d_bkh1 = b.dma("sync", bkh[0:32, :], bdkv_h.rearrange("(p o) -> p o", o=1), slow=True)
    d_bkh2 = b.dma("sync", bkh[32:64, :], bkr_h.rearrange("(p o) -> p o", o=1), slow=True)
    d_buq = b.dma("sync", buq_f[:], buq_h.rearrange("(h p) -> p h", p=128), slow=True)
    t_buq = b.v(lambda e: e.tensor_copy(buqb[:], buq_f[:]), d_buq)
    d_buv = b.dma("sync", buv_f[:], buv_h.rearrange("(h p) -> p h", p=128), slow=True)
    t_buv = b.v(lambda e: e.tensor_copy(buvb[:], buv_f[:]), d_buv)
    d_bqr = b.dma("gpsimd", bias64[32:64, :], bqr_h.rearrange("(h p) -> p h", p=32), slow=True)
    d_bo = b.dma("sync", bo_sb[:], bo_h.rearrange("(o c) -> o c", o=1))
    t_wo = []
    for dt in range(8):
        d = b.dma("sync", wo_st[dt % 2][:], wo_h[dt * 128:(dt + 1) * 128, :],
                  t_wo[dt - 2] if dt >= 2 else None)
        t_wo.append(b.v(lambda e, dt=dt: e.tensor_copy(wo_bf[:, dt * D:(dt + 1) * D],
                                                       wo_st[dt % 2][:]), d))
    d_uqT, d_ukT, d_uvT = [], [], []
    for h in range(H):
        d_uqT.append(b.dma("sync", wuqT[:, h * 32:(h + 1) * 32],
                           wuqb[:, h * 128:(h + 1) * 128], t_uq, transpose=True))
        d_ukT.append(b.dma("sync", wukT[:, h * 32:(h + 1) * 32],
                           wukb[:, h * 128:(h + 1) * 128], t_uk, transpose=True))
        d_uvT.append(b.dma("sync", wuvT[:, h * 32:(h + 1) * 32],
                           wuvb[:, h * 128:(h + 1) * 128], t_uv, transpose=True))

    d_x = [None] * NT
    s_ln1 = [None] * NT
    g_ln1 = [None] * NT
    wload_tok = [None] * 64
    wred_tok = [None] * 64
    win1_user = [None, None]
    wredu = [None]

    def p1_load(ci):
        wload_tok[ci] = b.dma("sync", w_in[ci % 2][:], chunk_src(ci), win1_user[ci % 2])

    def p1_red(ci):
        col = 0 if ci < 32 else 1
        r1 = b.v(lambda e: e.reduce_sum(wred[:], w_in[ci % 2][:], axis=AX.X,
                                        apply_absolute_value=True),
                 wload_tok[ci], wredu[0])
        r2 = b.v(lambda e, col=col: e.tensor_tensor(acc[:, col:col + 1],
                                                    acc[:, col:col + 1], wred[:],
                                                    ALU.add), r1)
        wred_tok[ci] = r2
        win1_user[ci % 2] = r2
        wredu[0] = r2

    wl_ct = [0]
    wr_ct = [0]

    def p1_pump(target):
        while wl_ct[0] < min(64, target):
            ci = wl_ct[0]
            if ci >= 2:
                p1_red(ci - 2)
                wr_ct[0] = ci - 1
            p1_load(ci)
            wl_ct[0] += 1

    for i in range(NT):
        d_x[i] = b.dma("sync", x_sb[i % 2][:], x_h[i * 128:(i + 1) * 128, :],
                       s_ln1[i - 2] if i >= 2 else None)
        p, st = pool[i % 2], stt[i % 2]
        v1 = b.v(lambda e, st=st, i=i: e.bn_stats(st[:, 0:6], x_sb[i % 2][:, 0:512]),
                 d_x[i], s_ln1[i - 2] if i >= 2 else None)
        v2 = b.v(lambda e, st=st, i=i: e.bn_stats(st[:, 6:12], x_sb[i % 2][:, 512:1024]), d_x[i])
        v3 = b.v(lambda e, st=st, p=p: e.bn_aggr(p[:, 0:2], st[:]), v1, v2)
        v4 = b.v(lambda e, p=p: e.tensor_scalar(p[:, 2:3], p[:, 1:2], LN_EPS, None, ALU.add), v3)
        s1 = b.s(lambda e, p=p: e.activation(p[:, 3:4], p[:, 2:3], AF.Ln), v4)
        s2 = b.s(lambda e, p=p: e.activation(p[:, 4:5], p[:, 3:4], AF.Exp, scale=-0.5), s1)
        v5 = b.v(lambda e, p=p: e.tensor_scalar(p[:, 5:6], p[:, 0:1], p[:, 4:5], -1.0,
                                                ALU.mult, ALU.mult), s2)
        s_ln1[i] = b.s(lambda e, p=p, i=i: e.activation(ln1b[i % 2][:], x_sb[i % 2][:],
                                                        AF.Identity, scale=p[:, 4:5],
                                                        bias=p[:, 5:6]),
                       v5, g_ln1[i - 2] if i >= 2 else None)
        g_ln1[i] = b.dma("gpsimd", ln1s[i * 128:(i + 1) * 128, :], ln1b[i % 2][:], s_ln1[i])
        p1_pump(-(-64 * (i + 1)) // NT)
    p1_pump(64)
    while wr_ct[0] < 64:
        p1_red(wr_ct[0])
        wr_ct[0] += 1
    wred_all = wred_tok
    gf = b.fence("gpsimd")
    d_lnT = [b.dma("sync", lnT[dt][:], ln1s[0:S, dt * 128:(dt + 1) * 128],
                   *gf, transpose=True) for dt in range(8)]

    t_khT_ev = [None] * NQ
    t_cqT_ev = [None] * NQ
    prev_kh = [None, None]
    prev_cq = [None, None]
    for blk in range(NQ):
        sl = slice(blk * 512, (blk + 1) * 512)
        mm = None
        for dt in range(8):
            mm = b.t(lambda e, dt=dt, blk=blk, sl=sl: e.matmul(
                pkh[blk % 2][:], wkh[:, dt * 64:(dt + 1) * 64], lnT[dt][:, sl],
                start=(dt == 0), stop=(dt == 7), skip_group_check=True),
                d_lnT[dt], t_wkh, prev_kh[blk % 2] if dt == 0 else None)
        t_khT_ev[blk] = b.v(lambda e, blk=blk, sl=sl: e.tensor_scalar(
            khT[0:64, sl], pkh[blk % 2][:], bkh[:], None, ALU.add), mm, d_bkh1, d_bkh2)
        prev_kh[blk % 2] = t_khT_ev[blk]
        mm = None
        for dt in range(8):
            mm = b.t(lambda e, dt=dt, blk=blk, sl=sl: e.matmul(
                pcq[blk % 2][:], wdqb[:, dt * 32:(dt + 1) * 32], lnT[dt][:, sl],
                start=(dt == 0), stop=(dt == 7), skip_group_check=True),
                d_lnT[dt], t_wdq, prev_cq[blk % 2] if dt == 0 else None)
        t_cqT_ev[blk] = b.v(lambda e, blk=blk, sl=sl: e.tensor_scalar(
            cqT[0:32, sl], pcq[blk % 2][:], bdq[:], None, ALU.add), mm, d_bdq, t_cq_z)
        prev_cq[blk % 2] = t_cqT_ev[blk]
    d_ckv = [b.dma("sync", ckv1[:, kt * 128:kt * 128 + 32],
                   khT[0:32, kt * 128:(kt + 1) * 128],
                   t_khT_ev[(kt * 128) // 512], t_ckv1_o, transpose=True)
             for kt in range(NK)]
    ckv_w = d_ckv[-8:]

    g_wc = []
    last_pmA = None
    last_pmB = None
    t_lwabs = []
    for h in range(H):
        mm = b.t(lambda e, h=h: e.matmul(pmA[0:32, 0:32], wuqT[:, h * 32:(h + 1) * 32],
                                         wukT[:, h * 32:(h + 1) * 32], start=True, stop=True),
                 d_uqT[h], d_ukT[h], last_pmA)
        last_pmA = b.v(lambda e, h=h: e.tensor_copy(lwabs[0:32, h * 64:h * 64 + 32],
                                                    pmA[0:32, 0:32]), mm, t_lw_z)
        t_lwabs.append(last_pmA)
        t_lwabs.append(b.v(lambda e, h=h: e.tensor_copy(
            lwabs[0:32, h * 64 + 32:h * 64 + 64], wqrb[:, h * 32:(h + 1) * 32]), t_qr))
        mm = b.t(lambda e, h=h: e.matmul(pmB[0:32, 0:1], wukT[:, h * 32:(h + 1) * 32],
                                         buqb[:, h:h + 1], start=True, stop=True),
                 d_ukT[h], t_buq, last_pmB)
        last_pmB = b.v(lambda e, h=h: e.tensor_copy(bias64[0:32, h:h + 1], pmB[0:32, 0:1]), mm)
        for ch in range(2):
            mm = b.t(lambda e, h=h, ch=ch: e.matmul(
                pmA[0:32, :], wuvT[:, h * 32:(h + 1) * 32],
                wo_bf[:, h * D + ch * 512:h * D + (ch + 1) * 512], start=True, stop=True),
                d_uvT[h], t_wo[h], last_pmA)
            last_pmA = b.v(lambda e, h=h, ch=ch: e.tensor_copy(
                wc_st[h % 2][:, ch * 512:(ch + 1) * 512], pmA[0:32, :]), mm,
                g_wc[h - 2] if h >= 2 else None)
        g_wc.append(b.dma("gpsimd", wc_pack[h // 4][(h % 4) * 32:(h % 4 + 1) * 32, :],
                          wc_st[h % 2][:], last_pmA))
    t_beff = []
    for ch in range(2):
        mm = None
        for kt in range(8):
            mm = b.t(lambda e, kt=kt, ch=ch: e.matmul(
                pmB[0:1, :], buvb[:, kt:kt + 1],
                wo_bf[:, kt * D + ch * 512:kt * D + (ch + 1) * 512],
                start=(kt == 0), stop=(kt == 7), skip_group_check=True),
                t_buv, t_wo[-1], last_pmB if kt == 0 else None)
        last_pmB = b.v(lambda e, ch=ch: e.tensor_tensor(
            beff_f[0:1, ch * 512:(ch + 1) * 512], pmB[0:1, :],
            bo_sb[0:1, ch * 512:(ch + 1) * 512], ALU.add), mm, d_bo)
        t_beff.append(last_pmB)
    t_beff_bf = b.v(lambda e: e.tensor_copy(beff[:], beff_f[:]), t_beff[0], t_beff[1])


    mm1 = b.t(lambda e: e.matmul(pmA[0:1, 0:2], accw[:], acc[:],
                                 start=True, stop=True, skip_group_check=True),
              wred_all[-1], *wred_all, t_accw, last_pmA)
    vv = b.v(lambda e: e.tensor_copy(tau_sb[:], pmA[0:1, 0:2]), mm1)
    mm2 = b.t(lambda e: e.matmul(pmA[0:128, 0:2], onesf_row[:], tau_sb[:],
                                 start=True, stop=True, skip_group_check=True),
              vv, t_onesf_row)
    v2f = b.v(lambda e: e.tensor_scalar(tau_p[:], pmA[0:128, 0:2],
                                        0.5 / float(DFF * D), None, ALU.mult), mm2)
    fin_tok = b.v(lambda e: e.tensor_scalar(tau_n[:], tau_p[:], -1.0, None, ALU.mult), v2f)

    b.barrier()
    scA.close()
    psA.close()
    scB = contextlib.ExitStack()
    sbB = lambda n, shp, dt=BF16: scB.enter_context(nc.sbuf_tensor(n, shp, dt))
    wuT = [sbB(f"wuT{d}", [128, DFF], BF16) for d in range(8)]
    qhT = [sbB(f"qhT{h}", [128, 512], BF16) for h in range(H)]
    t_qhT_z = [b.v(lambda e, h=h: e.memset(qhT[h][64:128, :], 0.0)) for h in range(H)]
    pT = [sbB(f"pT{i}", [128, 1024], BF16) for i in range(2)]
    ovtn = [[sbB(f"ovtn{q}_{j}", [128, 512], BF16) for j in range(2)] for q in range(2)]
    ovst = [sbB(f"ovst{i}", [32, 512], BF16) for i in range(2)]
    rl_bf = sbB("rl_bf", [1, 512], BF16)
    rl_f = sbB("rl_f", [1, 512], F32)
    rb_sb = sbB("rb_sb", [32, 512], BF16)
    attn_sb = [sbB("attn0", [128, D], F32)] * 2
    xn_bf = [sbB("xnbf0", [128, D], BF16)] * 2
    hT_sb = [sbB(f"hT{i}", [128, 32 * 256], BF16) for i in range(2)]
    NRING = 2
    wdR = [sbB(f"wdR{r}", [128, D], BF16) for r in range(NRING)]
    g2 = [sbB(f"g2_{i}", [128, 256], BF16) for i in range(2)]
    xnT_in = [sbB(f"xnT{i}", [128, 8 * 256], BF16) for i in range(2)]
    x2_in = [sbB(f"x2in{i}", [128, D], F32) for i in range(2)]
    outt = [sbB(f"outt{i}", [128, D], F32) for i in range(2)]
    ssq = sbB("ssq", [128, 2], F32)
    lssq = sbB("lssq", [128, 2], F32)
    sqs = sbB("sqs", [128, 2], F32)
    dens = sbB("dens", [128, 2], F32)
    scaleT = [sbB(f"scaleT{i}", [128, 2], F32) for i in range(2)]
    pool2 = [sbB(f"pool2_{i}", [128, 16], F32) for i in range(2)]
    stt2 = [sbB(f"stt2_{i}", [128, 12], F32) for i in range(2)]

    psB = contextlib.ExitStack()
    arena = [psB.enter_context(nc.psum_tensor(f"arena{i}", [128, 1024], F32)) for i in range(2)]
    povT = psB.enter_context(nc.psum_tensor("povT", [128, 512], F32))
    spare = psB.enter_context(nc.psum_tensor("spare", [128, 512], F32))
    ph = [psB.enter_context(nc.psum_tensor(f"ph{i}", [128, 512], F32)) for i in range(2)]

    bank = {"spare": None, "povT": None}
    arena_free = [None, None]
    busers = {"ph0": None, "ph1": None, "spare": None}
    bmap = {"ph0": ph[0], "ph1": ph[1], "spare": spare}
    util_ptr = [0]
    up_ptr = [0]
    dummy_tok = [None]

    def util_next():
        k = ("spare", "ph1")[util_ptr[0]]
        util_ptr[0] ^= 1
        return k

    def up_next():
        k = ("ph0", "ph1")[up_ptr[0]]
        up_ptr[0] ^= 1
        return k

    NG2 = NK // 2

    ws = {"load": [None] * 64, "tern": [None] * 64, "write": [None] * 64,
          "win": [wred_tok[62], wred_tok[63]], "wtu": [None] * 2, "tmpu": None}

    def chunk_src(i):
        if i < 32:
            return wup_h[i * 128:(i + 1) * 128, :]
        j = i - 32
        return wdn_h[(j // 4) * 128:(j // 4 + 1) * 128, (j % 4) * D:(j % 4 + 1) * D]

    def chunk_dst(i):
        if i < 32:
            return wups[i * 128:(i + 1) * 128, :]
        j = i - 32
        return wdns[(j // 4) * 128:(j // 4 + 1) * 128, (j % 4) * D:(j % 4 + 1) * D]

    def mk_p2_load(i):
        def f():
            ws["load"][i] = b.dma("sync", w_in[i % 2][:], chunk_src(i), ws["win"][i % 2])
        return f

    def mk_p2_tern(i):
        def f():
            col = 0 if i < 32 else 1
            t1 = b.v(lambda e, col=col: e.tensor_scalar(
                wt_tmp[:], w_in[i % 2][:], tau_n[:, col:col + 1], None, ALU.is_lt),
                ws["load"][i], fin_tok, ws["tmpu"], ws["wtu"][i % 2])
            t2 = b.v(lambda e, col=col: e.scalar_tensor_tensor(
                wt_bf[i % 2][:], w_in[i % 2][:], tau_p[:, col:col + 1], wt_tmp[:],
                ALU.is_gt, ALU.subtract), t1)
            ws["win"][i % 2] = t2
            ws["tmpu"] = t2
            g1 = b.dma("gpsimd", chunk_dst(i), wt_bf[i % 2][:], t2)
            ws["write"][i] = g1
            ws["wtu"][i % 2] = g1
        return f

    d_wuT = [None] * 8

    def mk_read_up(dt):
        def f():
            d_wuT[dt] = b.dma("sync", wuT[dt][:], wups[0:DFF, dt * 128:(dt + 1) * 128],
                              *ws["write"][0:32], transpose=True)
        return f

    wtasks = []
    for i in range(64):
        if i >= 2:
            wtasks.append(mk_p2_tern(i - 2))
        wtasks.append(mk_p2_load(i))
    for i in range(62, 64):
        wtasks.append(mk_p2_tern(i))
    for dt in range(8):
        wtasks.append(mk_read_up(dt))
    wti = [0]

    def pump_wtasks(n):
        k = 0
        while wti[0] < len(wtasks) and k < n:
            wtasks[wti[0]]()
            wti[0] += 1
            k += 1

    d_xnT_toks = [None, None]
    xnT_user = [None, None]
    hT_user = [None, None]
    hev = {}
    gelu_tok = {}
    g2_user = [None] * 2
    sq_last = {}
    ss_last = {}
    scaleT_tok = {}
    scaleT_user = [None, None]
    ring_user = [None] * NRING
    ring_tok = {}
    x2i_user = [None, None]
    outt_user = [None, None]
    p2pool_user = [None, None]
    p2stt_user = [None, None]
    g_out = []

    def mk_xnt_reads(tb):
        def f():
            par = (tb // 1) % 2
            toks = []
            for dt in range(8):
                toks.append(b.dma(
                    "sync",
                    xnT_in[par][:, dt * 256:(dt + 1) * 256],
                    xns[tb * 256:(tb + 1) * 256, dt * 128:(dt + 1) * 128],
                    g_xn[tb * 2], g_xn[tb * 2 + 1],
                    xnT_user[par] if dt == 0 else None, transpose=True))
            d_xnT_toks[par] = toks
        return f

    def mk_up(tb, ft):
        def f():
            par = tb % 2
            pk = up_next()
            pb = bmap[pk]
            mm = None
            for dt in range(8):
                mm = b.t(lambda e, dt=dt, ft=ft, par=par, pb=pb: e.matmul(
                    pb[:, 0:256], wuT[dt][:, ft * 128:(ft + 1) * 128],
                    xnT_in[par][:, dt * 256:(dt + 1) * 256],
                    start=(dt == 0), stop=(dt == 7), skip_group_check=True),
                    d_xnT_toks[par][dt] if ft == 0 else None,
                    d_wuT[dt],
                    busers[pk] if dt == 0 else None)
            ge = b.s(lambda e, ft=ft, par=par, pb=pb: e.activation(
                hT_sb[par][:, ft * 256:(ft + 1) * 256], pb[:, 0:256], AF.Gelu),
                mm, hT_user[par] if ft == 0 else None)
            busers[pk] = ge
            hev[(tb, ft)] = ge
            if ft == 31:
                xnT_user[par] = mm
        return f

    def mk_sqss(tb, ft):
        def f():
            par = tb % 2
            sq = b.v(lambda e, ft=ft, par=par: e.tensor_mul(
                g2[ft % 2][:], hT_sb[par][:, ft * 256:(ft + 1) * 256],
                hT_sb[par][:, ft * 256:(ft + 1) * 256]),
                hev[(tb, ft)], g2_user[ft % 2])
            tok = None
            for ts2 in range(2):
                tok = b.t(lambda e, ft=ft, ts2=ts2: e.matmul(
                    spare[:, ts2:ts2 + 1], g2[ft % 2][:, ts2 * 128:(ts2 + 1) * 128],
                    ones_col[:], start=(ft == 0 and ts2 == 0),
                    stop=(ft == 31 and ts2 == 1), skip_group_check=True),
                    sq, t_ones_col,
                    busers["spare"] if (ft == 0 and ts2 == 0) else None)
            g2_user[ft % 2] = tok
            ss_last[tb] = tok
        return f

    def mk_rms(tb, idx):
        def f():
            c1 = b.v(lambda e: e.tensor_copy(ssq[:], spare[:, 0:2]), ss_last[tb],
                     scaleT_user[idx])
            bank["spare"] = c1
            busers["spare"] = c1
            c2 = b.s(lambda e: e.activation(lssq[:], ssq[:], AF.Ln), c1)
            c3 = b.s(lambda e: e.activation(sqs[:], lssq[:], AF.Exp, scale=0.5), c2)
            c4 = b.v(lambda e: e.tensor_scalar(dens[:], sqs[:], 1.0 / 64.0, RMS_EPS,
                                               ALU.mult, ALU.add), c3)
            c5 = b.v(lambda e, idx=idx: e.reciprocal(scaleT[idx][:], dens[:]), c4)
            scaleT_tok[tb] = c5
        return f

    def mk_down_pair(ta, tb_):
        def f():
            accs = [ph[0][:], ph[1][:],
                    arena[0][:, 0:512], arena[0][:, 512:1024],
                    arena[1][:, 0:512], arena[1][:, 512:1024],
                    povT[:], spare[:]]
            acc_w = [busers["ph0"], busers["ph1"],
                     arena_free[0], arena_free[0], arena_free[1], arena_free[1],
                     bank["povT"], busers["spare"]]
            d_x2 = [None] * 4
            ring_tok = [None] * 32

            def ring_read(ft):
                r = ft % NRING
                ring_tok[ft] = b.dma("sync", wdR[r][:],
                                     wdns[0:D, ft * 128:(ft + 1) * 128],
                                     *ws["write"][32:64], ring_user[r], transpose=True)

            for ft in range(NRING):
                ring_read(ft)
            mmj = [None] * 8
            for ft in range(32):
                for j in range(8):
                    tb = (ta, tb_)[j // 4]
                    par = tb % 2
                    tsub = (j // 2) % 2
                    ch = j % 2
                    mmj[j] = b.t(lambda e, ft=ft, j=j, par=par, tsub=tsub, ch=ch: e.matmul(
                        accs[j],
                        hT_sb[par][:, ft * 256 + tsub * 128:ft * 256 + (tsub + 1) * 128],
                        wdR[ft % NRING][:, ch * 512:(ch + 1) * 512],
                        start=(ft == 0), stop=(ft == 31), skip_group_check=True),
                        hev[(tb, ft)] if j % 4 == 0 else None,
                        ring_tok[ft] if j == 0 else None,
                        acc_w[j] if ft == 0 else None)
                ring_user[ft % NRING] = mmj[7]
                if ft + NRING < 32:
                    ring_read(ft + NRING)
            hT_user[ta % 2] = mmj[3]
            hT_user[tb_ % 2] = mmj[7]
            evs = [None] * 8
            for q in range(4):
                tb = (ta, tb_)[q // 2]
                tsub = q % 2
                idx = q // 2
                gt = tb * 2 + tsub
                p, st = pool2[q % 2], stt2[q % 2]
                d_x2[q] = b.dma("gpsimd", x2_in[q % 2][:],
                                x2s[tb * 256 + tsub * 128:tb * 256 + (tsub + 1) * 128, :],
                                g_x2[tb * 2 + tsub], x2i_user[q % 2])
                for ch in range(2):
                    j = q * 2 + ch
                    evs[j] = b.v(lambda e, q=q, ch=ch, j=j, idx=idx, tsub=tsub: e.tensor_scalar(
                        outt[q % 2][:, ch * 512:(ch + 1) * 512], accs[j],
                        scaleT[idx][:, tsub:tsub + 1], None, ALU.mult),
                        mmj[7], scaleT_tok[tb], outt_user[q % 2] if ch == 0 else None)
                v1 = b.v(lambda e, st=st, q=q: e.bn_stats(
                    st[:, 0:6], outt[q % 2][:, 0:512]), evs[q * 2], p2stt_user[q % 2])
                v2 = b.v(lambda e, st=st, q=q: e.bn_stats(
                    st[:, 6:12], outt[q % 2][:, 512:1024]), evs[q * 2 + 1])
                v3 = b.v(lambda e, st=st, p=p: e.bn_aggr(p[:, 0:2], st[:]), v1, v2,
                         p2pool_user[q % 2])
                p2stt_user[q % 2] = v3
                v4 = b.v(lambda e, p=p: e.tensor_scalar(p[:, 2:3], p[:, 1:2], LN_EPS,
                                                        None, ALU.add), v3)
                s1 = b.s(lambda e, p=p: e.activation(p[:, 3:4], p[:, 2:3], AF.Ln), v4)
                s2 = b.s(lambda e, p=p: e.activation(p[:, 4:5], p[:, 3:4], AF.Exp,
                                                     scale=-0.5), s1)
                v5 = b.v(lambda e, p=p: e.tensor_scalar(p[:, 5:6], p[:, 0:1], p[:, 4:5],
                                                        -1.0, ALU.mult, ALU.mult), s2)
                v6 = b.v(lambda e, q=q, p=p: e.tensor_scalar(
                    outt[q % 2][:], outt[q % 2][:], p[:, 0:1], p[:, 4:5],
                    ALU.subtract, ALU.mult), v5)
                p2pool_user[q % 2] = v6
                v7 = b.v(lambda e, q=q: e.tensor_tensor(
                    outt[q % 2][:], outt[q % 2][:], x2_in[q % 2][:], ALU.add),
                    v6, d_x2[q])
                x2i_user[q % 2] = v7
                gdo = b.dma("gpsimd", out_h[gt * 128:(gt + 1) * 128, :], outt[q % 2][:], v7)
                outt_user[q % 2] = gdo
                g_out.append(gdo)
            busers["ph0"] = evs[0]
            busers["ph1"] = evs[1]
            arena_free[0] = evs[3]
            arena_free[1] = evs[5]
            bank["povT"] = evs[6]
            busers["spare"] = evs[7]
        return f

    def ffn_work(tbs):
        work = []
        ta, tb_ = tbs
        work.append(("fine", mk_xnt_reads(ta)))
        for ft in range(32):
            work.append(("fine", mk_up(ta, ft)))
            if ft == 16:
                work.append(("fine", mk_xnt_reads(tb_)))
            if ft >= 2:
                work.append(("fine", mk_sqss(ta, ft - 2)))
        work.append(("fine", mk_sqss(ta, 30)))
        work.append(("fine", mk_sqss(ta, 31)))
        work.append(("fine", mk_rms(ta, 0)))
        for ft in range(32):
            work.append(("fine", mk_up(tb_, ft)))
            if ft >= 2:
                work.append(("fine", mk_sqss(tb_, ft - 2)))
        work.append(("fine", mk_sqss(tb_, 30)))
        work.append(("fine", mk_sqss(tb_, 31)))
        work.append(("fine", mk_rms(tb_, 1)))
        work.append(("macro", mk_down_pair(ta, tb_)))
        return work

    arena_exp = arena_free
    pT_user = [None, None]
    ovst_user = [None, None]
    ovtn_user = [[None, None], [None, None]]
    attn_user = AliasTracker()
    xn_user = AliasTracker()
    xload = [None] * NT
    g_x2 = [None] * NT
    g_xn = [None] * NT
    xsb_user = [[s_ln1[NT - 2]], [s_ln1[NT - 1]]]
    pool_user = [s_ln1[NT - 2], s_ln1[NT - 1]]

    work = []
    wk = [0]

    def fines_left():
        n = 0
        for i in range(wk[0], len(work)):
            if work[i][0] == "macro":
                n += 0
            else:
                n += 1
        return n

    def pump_fine(n):
        k = 0
        while wk[0] < len(work) and k < n:
            kind, fn = work[wk[0]]
            if kind == "macro":
                break
            fn()
            wk[0] += 1
            k += 1

    def pump_boundary():
        if wk[0] < len(work) and work[wk[0]][0] == "macro":
            work[wk[0]][1]()
            wk[0] += 1

    def pump_all():
        while wk[0] < len(work):
            work[wk[0]][1]()
            wk[0] += 1

    def mk_dummy():
        tok = b.t(lambda e: e.matmul(
            ph[0][:], ckv1[:, 0:128], khT[:, 0:512],
            start=True, stop=True, skip_group_check=True),
            t_khT_ev[0], d_ckv[0], t_khT_z, dummy_tok[0], busers["ph0"])
        dummy_tok[0] = tok
        busers["ph0"] = tok

    def mk_qb_tail(qb, ovn_gds):
        items = []
        qpar = qb % 2

        def mk_itemA(tt):
            def f():
                gt = qb * 4 + tt
                tsl = slice(tt * 128, (tt + 1) * 128)
                p, st = pool[tt % 2], stt[tt % 2]
                pw = pool_user[tt % 2]
                pks = [None, None]
                mms = [None, None]
                for ch in range(2):
                    csl = slice(ch * 512, (ch + 1) * 512)
                    pk = util_next()
                    pb = bmap[pk]
                    pks[ch] = pk
                    b.t(lambda e, tsl=tsl, csl=csl, pb=pb: e.matmul(
                        pb[:], ovtn[qpar][0][:, tsl], wc_pack[0][:, csl],
                        start=True, stop=False, skip_group_check=True),
                        *ovn_gds, *g_wc, busers[pk])
                    b.t(lambda e, tsl=tsl, csl=csl, pb=pb: e.matmul(
                        pb[:], ovtn[qpar][1][:, tsl], wc_pack[1][:, csl],
                        start=False, stop=False, skip_group_check=True))
                    mms[ch] = b.t(lambda e, csl=csl, pb=pb: e.matmul(
                        pb[:], ones_row[0:1, :], beff[0:1, csl],
                        start=False, stop=True, skip_group_check=True), t_beff_bf)
                    ovtn_user[qpar][0] = mms[ch]
                    ovtn_user[qpar][1] = mms[ch]
                v1 = b.v(lambda e, st=st: e.bn_stats(st[:, 0:6], bmap[pks[0]][:]),
                         mms[0], pw)
                v2 = b.v(lambda e, st=st: e.bn_stats(st[:, 6:12], bmap[pks[1]][:]), mms[1])
                v3 = b.v(lambda e, st=st, p=p: e.bn_aggr(p[:, 0:2], st[:]), v1, v2)
                v4 = b.v(lambda e, p=p: e.tensor_scalar(p[:, 2:3], p[:, 1:2], LN_EPS, None,
                                                        ALU.add), v3)
                s1 = b.s(lambda e, p=p: e.activation(p[:, 3:4], p[:, 2:3], AF.Ln), v4)
                s2 = b.s(lambda e, p=p: e.activation(p[:, 4:5], p[:, 3:4], AF.Exp,
                                                     scale=-0.5), s1)
                v5 = b.v(lambda e, p=p: e.tensor_scalar(p[:, 5:6], p[:, 0:1], p[:, 4:5],
                                                        -1.0, ALU.mult, ALU.mult), s2)
                ap0 = b.v(lambda e, tt=tt, p=p: e.tensor_scalar(
                    attn_sb[tt % 2][:, 0:512], bmap[pks[0]][:], p[:, 0:1], p[:, 4:5],
                    ALU.subtract, ALU.mult), v5, attn_user[tt % 2])
                ap1 = b.v(lambda e, tt=tt, p=p: e.tensor_scalar(
                    attn_sb[tt % 2][:, 512:1024], bmap[pks[1]][:], p[:, 0:1], p[:, 4:5],
                    ALU.subtract, ALU.mult), v5)
                busers[pks[0]] = ap0
                busers[pks[1]] = ap1
                v7 = b.v(lambda e, tt=tt, gt=gt: e.tensor_tensor(
                    x_sb[gt % 2][:], attn_sb[tt % 2][:], x_sb[gt % 2][:], ALU.add),
                    ap0, ap1, xload[gt])
                attn_user[tt % 2] = v7
                g_x2[gt] = b.dma("gpsimd", x2s[gt * 128:(gt + 1) * 128, :],
                                 x_sb[gt % 2][:], v7)
                tailst["v7"][tt] = v7
                tailst["v3"][tt] = v3
            return f

        def mk_itemB(tt):
            def f():
                gt = qb * 4 + tt
                p, st = pool[tt % 2], stt[tt % 2]
                v7 = tailst["v7"][tt]
                u1 = b.v(lambda e, st=st, gt=gt: e.bn_stats(st[:, 0:6],
                                                            x_sb[gt % 2][:, 0:512]),
                         v7, tailst["v3"][tt])
                u2 = b.v(lambda e, st=st, gt=gt: e.bn_stats(st[:, 6:12],
                                                            x_sb[gt % 2][:, 512:1024]), v7)
                u3 = b.v(lambda e, st=st, p=p: e.bn_aggr(p[:, 6:8], st[:]), u1, u2)
                u4 = b.v(lambda e, p=p: e.tensor_scalar(p[:, 8:9], p[:, 7:8], LN_EPS, None,
                                                        ALU.add), u3)
                w1 = b.s(lambda e, p=p: e.activation(p[:, 9:11], p[:, 7:9], AF.Ln), u4)
                w2 = b.s(lambda e, p=p: e.activation(p[:, 11:13], p[:, 9:11], AF.Exp,
                                                     scale=0.5), w1)
                u5 = b.v(lambda e, p=p: e.scalar_tensor_tensor(p[:, 13:14], p[:, 12:13],
                                                               RMS_EPS, p[:, 11:12],
                                                               ALU.mult, ALU.add), w2)
                u6 = b.v(lambda e, p=p: e.reciprocal(p[:, 14:15], p[:, 13:14]), u5)
                u7 = b.v(lambda e, p=p: e.tensor_scalar(p[:, 15:16], p[:, 6:7], p[:, 14:15],
                                                        -1.0, ALU.mult, ALU.mult), u6)
                sxn = b.v(lambda e, tt=tt, gt=gt, p=p: e.tensor_scalar(
                    xn_bf[tt % 2][:], x_sb[gt % 2][:], p[:, 6:7], p[:, 14:15],
                    ALU.subtract, ALU.mult), u7, xn_user[tt % 2])
                pool_user[tt % 2] = u7
                g_xn[gt] = b.dma("gpsimd", xns[gt * 128:(gt + 1) * 128, :],
                                 xn_bf[tt % 2][:], sxn)
                xn_user[tt % 2] = g_xn[gt]
                xsb_user[gt % 2] = [sxn, g_x2[gt]]
                g2l = gt + 2
                if g2l < NT:
                    xload[g2l] = b.dma("sync", x_sb[g2l % 2][:],
                                       x_h[g2l * 128:(g2l + 1) * 128, :],
                                       *xsb_user[g2l % 2])
            return f

        for tt in range(4):
            items.append(("fine", mk_itemA(tt)))
            items.append(("fine", mk_itemB(tt)))
        return items


    def weave(tail_items, ffn_items):
        if not tail_items:
            return ffn_items
        cut = 34
        return (tail_items[0:4] + ffn_items[0:1]
                + tail_items[4:8] + ffn_items[1:cut]
                + ffn_items[cut:])

    tailst = {"v7": [None] * 4, "v3": [None] * 4}
    pending_tail = []

    for qb in range(NQ):
        assert wk[0] == len(work), f"ffn backlog {len(work)-wk[0]} at qb{qb}"
        work = pending_tail
        pending_tail = []
        wk[0] = 0
        qsl = slice(qb * 512, (qb + 1) * 512)
        if qb == 0:
            for tt in range(2):
                xload[tt] = b.dma("sync", x_sb[tt % 2][:], x_h[tt * 128:(tt + 1) * 128, :],
                                  *xsb_user[tt % 2])
        qh_ev = [None] * H
        for h in range(H):
            pk = util_next()
            pb = bmap[pk]
            mmq = b.t(lambda e, h=h, pb=pb, qsl=qsl: e.matmul(
                pb[0:64, :], lwabs[:, h * 64:(h + 1) * 64], cqT[:, qsl],
                start=True, stop=True, skip_group_check=True),
                t_cqT_ev[qb], busers[pk], t_lwabs[-1])
            qh_ev[h] = b.v(lambda e, h=h, pb=pb: e.tensor_scalar(
                qhT[h][0:64, :], pb[0:64, :], bias64[:, h:h + 1], None, ALU.add),
                mmq, last_pmB, d_bqr, t_qhT_z[h])
            busers[pk] = qh_ev[h]

        pend_tail = None
        ovn_gds = []
        for h in range(H):
            sc_tok = [None] * NG2
            e_tok = [None] * NG2

            def emit_S(g, h=h, qb=qb):
                tok = None
                for j in range(2):
                    kt = 2 * g + j
                    tok = b.t(lambda e, kt=kt, j=j, g=g, h=h: e.matmul(
                        arena[g % 2][:, j * 512:(j + 1) * 512],
                        khT[:, kt * 128:(kt + 1) * 128], qhT[h][:],
                        start=True, stop=True, skip_group_check=True),
                        qh_ev[h] if j == 0 else None,
                        t_khT_ev[-1] if (h == 0 and qb == 0 and j == 0) else None,
                        arena_free[g % 2] if j == 0 else None)
                sc_tok[g] = tok

            def emit_E(g, h=h):
                e_tok[g] = b.s(lambda e, g=g: e.activation(
                    pT[g % 2][:], arena[g % 2][:], AF.Exp, scale=SCALE),
                    sc_tok[g], pT_user[g % 2])
                arena_free[g % 2] = e_tok[g]

            def emit_O(g, h=h):
                tok = None
                for j in range(2):
                    kt = 2 * g + j
                    tok = b.t(lambda e, kt=kt, j=j, g=g: e.matmul(
                        povT[:], ckv1[:, kt * 128:(kt + 1) * 128],
                        pT[g % 2][:, j * 512:(j + 1) * 512],
                        start=(kt == 0), stop=(kt == NK - 1), skip_group_check=True),
                        e_tok[g] if j == 0 else None,
                        *(ckv_w if kt == 0 else []),
                        bank["povT"] if kt == 0 else None)
                pT_user[g % 2] = tok
                return tok

            emit_S(0)
            emit_E(0)
            if pend_tail is not None:
                pend_tail()
                pend_tail = None
            if qb == 0:
                pump_wtasks(-(-(len(wtasks) - wti[0]) // (H - h)))
            mk_dummy()
            for g in range(1, NG2):
                emit_S(g)
                emit_E(g)
                mk_dummy()
                emit_O(g - 1)
            last_ov = emit_O(NG2 - 1)

            def mk_tail(h=h, last_ov=last_ov, qb=qb):
                def f():
                    rll = b.s(lambda e: e.activation(rl_f[:], povT[32:33, :], AF.Ln),
                              last_ov)
                    rle = b.s(lambda e: e.activation(rl_bf[:], rl_f[:], AF.Exp,
                                                     scale=-1.0), rll)
                    pk = util_next()
                    pb = bmap[pk]
                    rb = b.t(lambda e, pb=pb: e.matmul(
                        pb[0:32, :], ones_row[0:1, 0:32], rl_bf[:],
                        start=True, stop=True, skip_group_check=True),
                        rle, t_ones_row, busers[pk])
                    rbe = b.v(lambda e, pb=pb: e.tensor_copy(rb_sb[:], pb[0:32, :]),
                              rb)
                    busers[pk] = rbe
                    ovn = b.v(lambda e, h=h: e.tensor_tensor(
                        ovst[h % 2][:], povT[0:32, :], rb_sb[:], ALU.mult),
                        rbe, ovst_user[h % 2])
                    bank["povT"] = ovn
                    gd = b.dma("gpsimd", ovtn[qb % 2][h // 4][(h % 4) * 32:(h % 4 + 1) * 32, :],
                               ovst[h % 2][:], ovn, ovtn_user[qb % 2][h // 4])
                    ovst_user[h % 2] = gd
                    ovn_gds.append(gd)
                    return gd
                return f

            pend_tail = mk_tail()
            if qb != 0:
                pump_fine(1)
        pend_tail()
        if qb == 0:
            while wti[0] < len(wtasks):
                pump_wtasks(8)
        else:
            pump_all()
        pending_tail = mk_qb_tail(qb, ovn_gds)



    pairs = [ffn_work((2 * j, 2 * j + 1)) for j in range(NQ)]
    for j in range(NQ - 1):
        nxt = pairs[j + 1].pop(0)
        pairs[j].insert(len(pairs[j]) - 1, nxt)
    work = list(pending_tail)
    for pr in pairs:
        work += pr
    wk[0] = 0
    pump_all()

    gfinal = [(b.sem[k], b.c[k]) for k in
              [f"g{i}" for i in range(4)] + [f"e{i}" for i in range(4)]]

    def _final_wait(e, gfinal=gfinal):
        for smh, val in gfinal:
            e.wait_ge(smh, val)
    b.raw("gpsimd", _final_wait)

    with nc.allow_low_precision(reason="bf16 matmul operands per problem tolerance spec"):
        with nc.Block() as block:
            @block.tensor
            def _(eng):
                for f in b.q["tensor"]:
                    f(eng)

            @block.vector
            def _(eng):
                for f in b.q["vector"]:
                    f(eng)

            @block.scalar
            def _(eng):
                for f in b.q["scalar"]:
                    f(eng)

            @block.sync
            def _(eng):
                for f in b.q["sync"]:
                    f(eng)

            @block.gpsimd
            def _(eng):
                for f in b.q["gpsimd"]:
                    f(eng)

    psB.close()
    scB.close()
    stack.close()
    return nc


def kernel(**inputs):
    x = np.ascontiguousarray(np.asarray(inputs["L0_attn_h_t"], dtype=np.float32))
    B_, S, D_ = x.shape
    wmap = {
        "W_DQ": "L0_attn_W_DQ", "b_DQ": "L0_attn_b_DQ",
        "W_UQ": "L0_attn_W_UQ", "b_UQ": "L0_attn_b_UQ",
        "W_DKV": "L0_attn_W_DKV", "b_DKV": "L0_attn_b_DKV",
        "W_UK": "L0_attn_W_UK",
        "W_UV": "L0_attn_W_UV", "b_UV": "L0_attn_b_UV",
        "W_QR": "L0_attn_W_QR", "b_QR": "L0_attn_b_QR",
        "W_KR": "L0_attn_W_KR", "b_KR": "L0_attn_b_KR",
        "W_O": "L0_attn_W_O", "b_O": "L0_attn_b_O",
        "w_up": "L0_ffn_w_up", "w_down": "L0_ffn_w_down",
    }
    wd = {k: np.ascontiguousarray(np.asarray(inputs[v], dtype=np.float32))
          for k, v in wmap.items()}
    nc = build_nc(S)
    in_maps = [dict(x=x[i], **wd) for i in range(B_)]
    res = run_bass_kernel_spmd(nc, in_maps, core_ids=list(range(B_)))
    out = np.stack([np.asarray(r["out"], dtype=np.float32) for r in res.results], axis=0)
    return out


# version 24
# speedup vs baseline: 1.0094x; 1.0094x over previous
import numpy as np
import contextlib
import concourse.bass as bass
import concourse.mybir as mybir
from concourse.bass_utils import run_bass_kernel_spmd

F32 = mybir.dt.float32
BF16 = mybir.dt.bfloat16
AF = mybir.ActivationFunctionType
ALU = mybir.AluOpType
AX = mybir.AxisListType

D = 1024
H = 8
DFF = 4096
SCALE = float(1.0 / np.sqrt(160.0))
LN_EPS = 1e-5
RMS_EPS = 1e-8
N_CORES = 8


class AliasTracker(list):

    def __init__(self):
        super().__init__([None])

    def __getitem__(self, i):
        return super().__getitem__(0)

    def __setitem__(self, i, v):
        super().__setitem__(0, v)


class Builder:
    ENGS = ("tensor", "vector", "scalar", "sync", "gpsimd")
    E2S = {"tensor": "t", "vector": "v", "scalar": "s", "sync": "d", "gpsimd": "g"}

    DK = {"sync": 8, "gpsimd": 4, "scalar": 4}

    def __init__(self, nc, stack):
        self.nc = nc
        self.q = {e: [] for e in self.ENGS}
        keys = (["t", "v", "s"] + [f"d{i}" for i in range(8)]
                + [f"g{i}" for i in range(4)] + [f"e{i}" for i in range(4)])
        self.c = {k: 0 for k in keys}
        self.sem = {k: stack.enter_context(nc.semaphore("sem_" + k)) for k in keys}
        self.hw = {e: {k: 0 for k in keys} for e in self.ENGS}
        self.pending = {e: [] for e in self.ENGS}
        self.dptr = {"sync": 0, "gpsimd": 0, "scalar": 0}

    def barrier(self):
        snap = [(k, self.c[k]) for k in self.c]
        for e in self.ENGS:
            self.pending[e] = list(snap)

    def _emit(self, eng, fn, waits, semkey=None):
        hw = self.hw[eng]
        wl = []
        allw = self.pending[eng] + [w for w in waits if w is not None]
        self.pending[eng] = []
        for k, val in allw:
            assert val <= self.c[k], f"wait on future {k}:{val} > {self.c[k]}"
            if val > hw[k]:
                hw[k] = val
                wl.append((self.sem[k], val))
        sk = semkey if semkey is not None else self.E2S[eng]
        inc = 16 if (len(sk) > 1 and sk[0] in ("d", "g", "e")) else 1
        self.c[sk] += inc
        tok = self.c[sk]
        sem = self.sem[sk]

        def run(e, fn=fn, wl=wl, sem=sem, inc=inc):
            for (h, v) in wl:
                e.wait_ge(h, v)
            fn(e).then_inc(sem, inc)

        self.q[eng].append(run)
        return (sk, tok)

    def t(self, fn, *waits):
        return self._emit("tensor", fn, waits)

    def v(self, fn, *waits):
        return self._emit("vector", fn, waits)

    def s(self, fn, *waits):
        return self._emit("scalar", fn, waits)

    def dma(self, eng, out, in_, *waits, transpose=False, slow=False):
        def fn(e, out=out, in_=in_, transpose=transpose, slow=slow):
            kw = {}
            if transpose:
                kw["transpose"] = True
            if slow:
                kw["allow_slow_non_contiguous"] = True
            return e.dma_start(out=out, in_=in_, **kw)
        K = self.DK[eng]
        ptr = self.dptr[eng]
        self.dptr[eng] = (ptr + 1) % K
        key = {"sync": "d", "gpsimd": "g", "scalar": "e"}[eng] + str(ptr)
        chain = (key, self.c[key])
        return self._emit(eng, fn, list(waits) + [chain], semkey=key)

    def fence(self, eng):
        pre = "d" if eng == "sync" else "g"
        K = self.DK[eng]
        return [(pre + str(i), self.c[pre + str(i)]) for i in range(K)]

    def raw(self, eng, fn):
        self.q[eng].append(fn)


def build_nc(S):
    NT = S // 128
    NQ = S // 512
    NK = S // 128
    TB = S // 256
    NG = -(-NK // 3)
    nc = bass.Bass("TRN2", target_bir_lowering=False)

    x_h = nc.dram_tensor("x", [S, D], F32, kind="ExternalInput").ap()
    out_h = nc.dram_tensor("out", [S, D], F32, kind="ExternalOutput").ap()
    wdq_h = nc.dram_tensor("W_DQ", [D, 32], F32, kind="ExternalInput").ap()
    bdq_h = nc.dram_tensor("b_DQ", [32], F32, kind="ExternalInput").ap()
    wuq_h = nc.dram_tensor("W_UQ", [32, D], F32, kind="ExternalInput").ap()
    buq_h = nc.dram_tensor("b_UQ", [D], F32, kind="ExternalInput").ap()
    wdkv_h = nc.dram_tensor("W_DKV", [D, 32], F32, kind="ExternalInput").ap()
    bdkv_h = nc.dram_tensor("b_DKV", [32], F32, kind="ExternalInput").ap()
    wuk_h = nc.dram_tensor("W_UK", [32, D], F32, kind="ExternalInput").ap()
    wuv_h = nc.dram_tensor("W_UV", [32, D], F32, kind="ExternalInput").ap()
    buv_h = nc.dram_tensor("b_UV", [D], F32, kind="ExternalInput").ap()
    wqr_h = nc.dram_tensor("W_QR", [32, 256], F32, kind="ExternalInput").ap()
    bqr_h = nc.dram_tensor("b_QR", [256], F32, kind="ExternalInput").ap()
    wkr_h = nc.dram_tensor("W_KR", [D, 32], F32, kind="ExternalInput").ap()
    bkr_h = nc.dram_tensor("b_KR", [32], F32, kind="ExternalInput").ap()
    wo_h = nc.dram_tensor("W_O", [D, D], F32, kind="ExternalInput").ap()
    bo_h = nc.dram_tensor("b_O", [D], F32, kind="ExternalInput").ap()
    wup_h = nc.dram_tensor("w_up", [DFF, D], F32, kind="ExternalInput").ap()
    wdn_h = nc.dram_tensor("w_down", [D, DFF], F32, kind="ExternalInput").ap()
    ln1s = nc.dram_tensor("ln1s", [S, D], BF16, kind="Internal").ap()
    xns = nc.dram_tensor("xns", [S, D], BF16, kind="Internal").ap()
    x2s = nc.dram_tensor("x2s", [S, D], F32, kind="Internal").ap()
    wups = nc.dram_tensor("wups", [DFF, D], BF16, kind="Internal").ap()
    wdns = nc.dram_tensor("wdns", [D, DFF], BF16, kind="Internal").ap()

    stack = contextlib.ExitStack()
    b = Builder(nc, stack)
    sb = lambda n, shp, dt=BF16: stack.enter_context(nc.sbuf_tensor(n, shp, dt))

    ones_row = sb("ones_row", [1, 128], BF16)
    ones_col = sb("ones_col", [128, 1], BF16)
    onesf_row = sb("onesf_row", [1, 128], F32)
    accw = sb("accw", [128, 1], F32)
    bkh = sb("bkh", [64, 1], F32)
    bdq = sb("bdq", [32, 1], F32)
    bias64 = sb("bias64", [64, 8], F32)
    beff = sb("beff", [1, D], BF16)
    wc_pack = [sb(f"wc{j}", [128, D], BF16) for j in range(2)]
    lwabs = sb("lwabs", [128, 512], BF16)
    cqT = sb("cqT", [128, S], BF16)
    khT = sb("khT", [128, S], BF16)
    ckv1 = sb("ckv1", [128, 128 * NK], BF16)
    tau_p = sb("tau_p", [128, 2], F32)
    tau_n = sb("tau_n", [128, 2], F32)
    acc = sb("acc", [128, 2], F32)
    x_sb = [sb(f"x_sb{i}", [128, D], F32) for i in range(2)]
    pool = [sb(f"pool{i}", [128, 16], F32) for i in range(2)]
    stt = [sb(f"stt{i}", [128, 12], F32) for i in range(2)]

    w_in = [sb(f"w_in{i}", [128, D], F32) for i in range(2)]
    wt_bf = [sb(f"wt_bf{i}", [128, D], BF16) for i in range(2)]
    wt_tmp = sb("wt_tmp", [128, D], BF16)
    wred = sb("wred", [128, 1], F32)
    tau_sb = sb("tau_sb", [1, 2], F32)

    def chunk_src(i):
        if i < 32:
            return wup_h[i * 128:(i + 1) * 128, :]
        j = i - 32
        return wdn_h[(j // 4) * 128:(j // 4 + 1) * 128, (j % 4) * D:(j % 4 + 1) * D]

    scA = contextlib.ExitStack()
    sbA = lambda n, shp, dt=BF16: scA.enter_context(nc.sbuf_tensor(n, shp, dt))
    lnT = [sbA(f"lnT{d}", [128, S], BF16) for d in range(8)]
    wo_st = [sbA(f"wo_st{i}", [128, D], F32) for i in range(2)]
    wo_bf = sbA("wo_bf", [128, 8 * D], BF16)
    wkh_f = sbA("wkh_f", [128, 8 * 64], F32)
    wkh = sbA("wkh", [128, 8 * 64], BF16)
    wdq_f = sbA("wdq_f", [128, 8 * 32], F32)
    wdqb = sbA("wdqb", [128, 8 * 32], BF16)
    wuq_f = sbA("wuq_f", [32, D], F32)
    wuqb = sbA("wuqb", [32, D], BF16)
    wuk_f = sbA("wuk_f", [32, D], F32)
    wukb = sbA("wukb", [32, D], BF16)
    wuv_f = sbA("wuv_f", [32, D], F32)
    wuvb = sbA("wuvb", [32, D], BF16)
    wqr_f = sbA("wqr_f", [32, 256], F32)
    wqrb = sbA("wqrb", [32, 256], BF16)
    wuqT = sbA("wuqT", [128, 8 * 32], BF16)
    wukT = sbA("wukT", [128, 8 * 32], BF16)
    wuvT = sbA("wuvT", [128, 8 * 32], BF16)
    buq_f = sbA("buq_f", [128, 8], F32)
    buqb = sbA("buqb", [128, 8], BF16)
    buv_f = sbA("buv_f", [128, 8], F32)
    buvb = sbA("buvb", [128, 8], BF16)
    bo_sb = sbA("bo_sb", [1, D], F32)
    beff_f = sbA("beff_f", [1, D], F32)
    ln1b = [sbA(f"ln1b{i}", [128, D], BF16) for i in range(2)]
    wc_st = [sbA(f"wc_st{i}", [32, D], BF16) for i in range(2)]

    psA = contextlib.ExitStack()
    pkh = [psA.enter_context(nc.psum_tensor(f"pkh{i}", [64, 512], F32)) for i in range(2)]
    pcq = [psA.enter_context(nc.psum_tensor(f"pcq{i}", [32, 512], F32)) for i in range(2)]
    pmA = psA.enter_context(nc.psum_tensor("pmA", [128, 512], F32))
    pmB = psA.enter_context(nc.psum_tensor("pmB", [128, 512], F32))

    t_ones_row = b.v(lambda e: e.memset(ones_row[:], 1.0))
    t_ones_col = b.v(lambda e: e.memset(ones_col[:], 1.0))
    t_onesf_row = b.v(lambda e: e.memset(onesf_row[:], 1.0))
    t_accw = b.v(lambda e: e.memset(accw[:], 1.0))
    b.v(lambda e: e.memset(acc[:], 0.0))
    t_ckv1_i = b.v(lambda e: e.memset(ckv1[:], 0.0))
    _cap = ckv1[:, 32:33]
    t_ckv1_o = b.v(lambda e: e.memset(
        bass.AP(_cap.tensor, _cap.offset, [[128 * NK, 128], [128, NK]]), 1.0), t_ckv1_i)
    t_khT_z = b.v(lambda e: e.memset(khT[64:128, :], 0.0))
    t_lw_za = b.v(lambda e: e.memset(lwabs[32:64, :], 0.0))
    t_lw_z = b.v(lambda e: e.memset(lwabs[64:128, :], 0.0), t_lw_za)
    t_cq_za = b.v(lambda e: e.memset(cqT[32:64, :], 0.0))
    t_cq_z = b.v(lambda e: e.memset(cqT[64:128, :], 0.0), t_cq_za)

    d_wkh = []
    for dt in range(8):
        d_wkh.append(b.dma("sync", wkh_f[:, dt * 64:dt * 64 + 32],
                           wdkv_h[dt * 128:(dt + 1) * 128, :]))
        d_wkh.append(b.dma("sync", wkh_f[:, dt * 64 + 32:dt * 64 + 64],
                           wkr_h[dt * 128:(dt + 1) * 128, :]))
    t_wkh = b.v(lambda e: e.tensor_copy(wkh[:], wkh_f[:]), *d_wkh)
    d_wdq = [b.dma("sync", wdq_f[:, dt * 32:(dt + 1) * 32],
                   wdq_h[dt * 128:(dt + 1) * 128, :]) for dt in range(8)]
    t_wdq = b.v(lambda e: e.tensor_copy(wdqb[:], wdq_f[:]), *d_wdq)
    d_uq = b.dma("sync", wuq_f[:], wuq_h[:])
    t_uq = b.v(lambda e: e.tensor_copy(wuqb[:], wuq_f[:]), d_uq)
    d_uk = b.dma("sync", wuk_f[:], wuk_h[:])
    t_uk = b.v(lambda e: e.tensor_copy(wukb[:], wuk_f[:]), d_uk)
    d_uv = b.dma("sync", wuv_f[:], wuv_h[:])
    t_uv = b.v(lambda e: e.tensor_copy(wuvb[:], wuv_f[:]), d_uv)
    d_qr = b.dma("sync", wqr_f[:], wqr_h[:])
    t_qr = b.v(lambda e: e.tensor_copy(wqrb[:], wqr_f[:]), d_qr)
    d_bdq = b.dma("sync", bdq[:], bdq_h.rearrange("(p o) -> p o", o=1), slow=True)
    d_bkh1 = b.dma("sync", bkh[0:32, :], bdkv_h.rearrange("(p o) -> p o", o=1), slow=True)
    d_bkh2 = b.dma("sync", bkh[32:64, :], bkr_h.rearrange("(p o) -> p o", o=1), slow=True)
    d_buq = b.dma("sync", buq_f[:], buq_h.rearrange("(h p) -> p h", p=128), slow=True)
    t_buq = b.v(lambda e: e.tensor_copy(buqb[:], buq_f[:]), d_buq)
    d_buv = b.dma("sync", buv_f[:], buv_h.rearrange("(h p) -> p h", p=128), slow=True)
    t_buv = b.v(lambda e: e.tensor_copy(buvb[:], buv_f[:]), d_buv)
    d_bqr = b.dma("gpsimd", bias64[32:64, :], bqr_h.rearrange("(h p) -> p h", p=32), slow=True)
    d_bo = b.dma("sync", bo_sb[:], bo_h.rearrange("(o c) -> o c", o=1))
    t_wo = []
    for dt in range(8):
        d = b.dma("sync", wo_st[dt % 2][:], wo_h[dt * 128:(dt + 1) * 128, :],
                  t_wo[dt - 2] if dt >= 2 else None)
        t_wo.append(b.v(lambda e, dt=dt: e.tensor_copy(wo_bf[:, dt * D:(dt + 1) * D],
                                                       wo_st[dt % 2][:]), d))
    d_uqT, d_ukT, d_uvT = [], [], []
    for h in range(H):
        d_uqT.append(b.dma("sync", wuqT[:, h * 32:(h + 1) * 32],
                           wuqb[:, h * 128:(h + 1) * 128], t_uq, transpose=True))
        d_ukT.append(b.dma("sync", wukT[:, h * 32:(h + 1) * 32],
                           wukb[:, h * 128:(h + 1) * 128], t_uk, transpose=True))
        d_uvT.append(b.dma("sync", wuvT[:, h * 32:(h + 1) * 32],
                           wuvb[:, h * 128:(h + 1) * 128], t_uv, transpose=True))

    d_x = [None] * NT
    s_ln1 = [None] * NT
    g_ln1 = [None] * NT
    wload_tok = [None] * 64
    wred_tok = [None] * 64
    win1_user = [None, None]
    wredu = [None]

    def p1_load(ci):
        wload_tok[ci] = b.dma("sync", w_in[ci % 2][:], chunk_src(ci), win1_user[ci % 2])

    def p1_red(ci):
        col = 0 if ci < 32 else 1
        r1 = b.v(lambda e: e.reduce_sum(wred[:], w_in[ci % 2][:], axis=AX.X,
                                        apply_absolute_value=True),
                 wload_tok[ci], wredu[0])
        r2 = b.v(lambda e, col=col: e.tensor_tensor(acc[:, col:col + 1],
                                                    acc[:, col:col + 1], wred[:],
                                                    ALU.add), r1)
        wred_tok[ci] = r2
        win1_user[ci % 2] = r2
        wredu[0] = r2

    wl_ct = [0]
    wr_ct = [0]

    def p1_pump(target):
        while wl_ct[0] < min(64, target):
            ci = wl_ct[0]
            if ci >= 2:
                p1_red(ci - 2)
                wr_ct[0] = ci - 1
            p1_load(ci)
            wl_ct[0] += 1

    for i in range(NT):
        d_x[i] = b.dma("sync", x_sb[i % 2][:], x_h[i * 128:(i + 1) * 128, :],
                       s_ln1[i - 2] if i >= 2 else None)
        p, st = pool[i % 2], stt[i % 2]
        v1 = b.v(lambda e, st=st, i=i: e.bn_stats(st[:, 0:6], x_sb[i % 2][:, 0:512]),
                 d_x[i], s_ln1[i - 2] if i >= 2 else None)
        v2 = b.v(lambda e, st=st, i=i: e.bn_stats(st[:, 6:12], x_sb[i % 2][:, 512:1024]), d_x[i])
        v3 = b.v(lambda e, st=st, p=p: e.bn_aggr(p[:, 0:2], st[:]), v1, v2)
        v4 = b.v(lambda e, p=p: e.tensor_scalar(p[:, 2:3], p[:, 1:2], LN_EPS, None, ALU.add), v3)
        s1 = b.s(lambda e, p=p: e.activation(p[:, 3:4], p[:, 2:3], AF.Ln), v4)
        s2 = b.s(lambda e, p=p: e.activation(p[:, 4:5], p[:, 3:4], AF.Exp, scale=-0.5), s1)
        v5 = b.v(lambda e, p=p: e.tensor_scalar(p[:, 5:6], p[:, 0:1], p[:, 4:5], -1.0,
                                                ALU.mult, ALU.mult), s2)
        s_ln1[i] = b.s(lambda e, p=p, i=i: e.activation(ln1b[i % 2][:], x_sb[i % 2][:],
                                                        AF.Identity, scale=p[:, 4:5],
                                                        bias=p[:, 5:6]),
                       v5, g_ln1[i - 2] if i >= 2 else None)
        g_ln1[i] = b.dma("gpsimd", ln1s[i * 128:(i + 1) * 128, :], ln1b[i % 2][:], s_ln1[i])
        p1_pump(-(-64 * (i + 1)) // NT)
    p1_pump(64)
    while wr_ct[0] < 64:
        p1_red(wr_ct[0])
        wr_ct[0] += 1
    wred_all = wred_tok
    gf = b.fence("gpsimd")
    d_lnT = [b.dma("sync", lnT[dt][:], ln1s[0:S, dt * 128:(dt + 1) * 128],
                   *gf, transpose=True) for dt in range(8)]

    t_khT_ev = [None] * NQ
    t_cqT_ev = [None] * NQ
    prev_kh = [None, None]
    prev_cq = [None, None]
    for blk in range(NQ):
        sl = slice(blk * 512, (blk + 1) * 512)
        mm = None
        for dt in range(8):
            mm = b.t(lambda e, dt=dt, blk=blk, sl=sl: e.matmul(
                pkh[blk % 2][:], wkh[:, dt * 64:(dt + 1) * 64], lnT[dt][:, sl],
                start=(dt == 0), stop=(dt == 7), skip_group_check=True),
                d_lnT[dt], t_wkh, prev_kh[blk % 2] if dt == 0 else None)
        t_khT_ev[blk] = b.v(lambda e, blk=blk, sl=sl: e.tensor_scalar(
            khT[0:64, sl], pkh[blk % 2][:], bkh[:], None, ALU.add), mm, d_bkh1, d_bkh2)
        prev_kh[blk % 2] = t_khT_ev[blk]
        mm = None
        for dt in range(8):
            mm = b.t(lambda e, dt=dt, blk=blk, sl=sl: e.matmul(
                pcq[blk % 2][:], wdqb[:, dt * 32:(dt + 1) * 32], lnT[dt][:, sl],
                start=(dt == 0), stop=(dt == 7), skip_group_check=True),
                d_lnT[dt], t_wdq, prev_cq[blk % 2] if dt == 0 else None)
        t_cqT_ev[blk] = b.v(lambda e, blk=blk, sl=sl: e.tensor_scalar(
            cqT[0:32, sl], pcq[blk % 2][:], bdq[:], None, ALU.add), mm, d_bdq, t_cq_z)
        prev_cq[blk % 2] = t_cqT_ev[blk]
    d_ckv = [b.dma("sync", ckv1[:, kt * 128:kt * 128 + 32],
                   khT[0:32, kt * 128:(kt + 1) * 128],
                   t_khT_ev[(kt * 128) // 512], t_ckv1_o, transpose=True)
             for kt in range(NK)]
    ckv_w = d_ckv[-8:]

    g_wc = []
    last_pmA = None
    last_pmB = None
    t_lwabs = []
    for h in range(H):
        mm = b.t(lambda e, h=h: e.matmul(pmA[0:32, 0:32], wuqT[:, h * 32:(h + 1) * 32],
                                         wukT[:, h * 32:(h + 1) * 32], start=True, stop=True),
                 d_uqT[h], d_ukT[h], last_pmA)
        last_pmA = b.v(lambda e, h=h: e.tensor_copy(lwabs[0:32, h * 64:h * 64 + 32],
                                                    pmA[0:32, 0:32]), mm, t_lw_z)
        t_lwabs.append(last_pmA)
        t_lwabs.append(b.v(lambda e, h=h: e.tensor_copy(
            lwabs[0:32, h * 64 + 32:h * 64 + 64], wqrb[:, h * 32:(h + 1) * 32]), t_qr))
        mm = b.t(lambda e, h=h: e.matmul(pmB[0:32, 0:1], wukT[:, h * 32:(h + 1) * 32],
                                         buqb[:, h:h + 1], start=True, stop=True),
                 d_ukT[h], t_buq, last_pmB)
        last_pmB = b.v(lambda e, h=h: e.tensor_copy(bias64[0:32, h:h + 1], pmB[0:32, 0:1]), mm)
        for ch in range(2):
            mm = b.t(lambda e, h=h, ch=ch: e.matmul(
                pmA[0:32, :], wuvT[:, h * 32:(h + 1) * 32],
                wo_bf[:, h * D + ch * 512:h * D + (ch + 1) * 512], start=True, stop=True),
                d_uvT[h], t_wo[h], last_pmA)
            last_pmA = b.v(lambda e, h=h, ch=ch: e.tensor_copy(
                wc_st[h % 2][:, ch * 512:(ch + 1) * 512], pmA[0:32, :]), mm,
                g_wc[h - 2] if h >= 2 else None)
        g_wc.append(b.dma("gpsimd", wc_pack[h // 4][(h % 4) * 32:(h % 4 + 1) * 32, :],
                          wc_st[h % 2][:], last_pmA))
    t_beff = []
    for ch in range(2):
        mm = None
        for kt in range(8):
            mm = b.t(lambda e, kt=kt, ch=ch: e.matmul(
                pmB[0:1, :], buvb[:, kt:kt + 1],
                wo_bf[:, kt * D + ch * 512:kt * D + (ch + 1) * 512],
                start=(kt == 0), stop=(kt == 7), skip_group_check=True),
                t_buv, t_wo[-1], last_pmB if kt == 0 else None)
        last_pmB = b.v(lambda e, ch=ch: e.tensor_tensor(
            beff_f[0:1, ch * 512:(ch + 1) * 512], pmB[0:1, :],
            bo_sb[0:1, ch * 512:(ch + 1) * 512], ALU.add), mm, d_bo)
        t_beff.append(last_pmB)
    t_beff_bf = b.v(lambda e: e.tensor_copy(beff[:], beff_f[:]), t_beff[0], t_beff[1])


    mm1 = b.t(lambda e: e.matmul(pmA[0:1, 0:2], accw[:], acc[:],
                                 start=True, stop=True, skip_group_check=True),
              wred_all[-1], *wred_all, t_accw, last_pmA)
    vv = b.v(lambda e: e.tensor_copy(tau_sb[:], pmA[0:1, 0:2]), mm1)
    mm2 = b.t(lambda e: e.matmul(pmA[0:128, 0:2], onesf_row[:], tau_sb[:],
                                 start=True, stop=True, skip_group_check=True),
              vv, t_onesf_row)
    v2f = b.v(lambda e: e.tensor_scalar(tau_p[:], pmA[0:128, 0:2],
                                        0.5 / float(DFF * D), None, ALU.mult), mm2)
    fin_tok = b.v(lambda e: e.tensor_scalar(tau_n[:], tau_p[:], -1.0, None, ALU.mult), v2f)

    b.barrier()
    scA.close()
    psA.close()
    scB = contextlib.ExitStack()
    sbB = lambda n, shp, dt=BF16: scB.enter_context(nc.sbuf_tensor(n, shp, dt))
    wuT = [sbB(f"wuT{d}", [128, DFF], BF16) for d in range(8)]
    qhT = [sbB(f"qhT{h}", [128, 512], BF16) for h in range(H)]
    t_qhT_z = [b.v(lambda e, h=h: e.memset(qhT[h][64:128, :], 0.0)) for h in range(H)]
    pT = [sbB(f"pT{i}", [128, 1024], BF16) for i in range(2)]
    ovtn = [[sbB(f"ovtn{q}_{j}", [128, 512], BF16) for j in range(2)] for q in range(2)]
    ovst = [sbB(f"ovst{i}", [32, 512], BF16) for i in range(2)]
    rl_bf = sbB("rl_bf", [1, 512], BF16)
    rl_f = sbB("rl_f", [1, 512], F32)
    rb_sb = sbB("rb_sb", [32, 512], BF16)
    attn_sb = [sbB("attn0", [128, D], F32)] * 2
    xn_bf = [sbB("xnbf0", [128, D], BF16)] * 2
    hT_sb = [sbB(f"hT{i}", [128, 32 * 256], BF16) for i in range(2)]
    NRING = 2
    wdR = [sbB(f"wdR{r}", [128, D], BF16) for r in range(NRING)]
    g2 = [sbB(f"g2_{i}", [128, 256], BF16) for i in range(2)]
    xnT_in = [sbB(f"xnT{i}", [128, 8 * 256], BF16) for i in range(2)]
    x2_in = [sbB(f"x2in{i}", [128, D], F32) for i in range(2)]
    outt = [sbB(f"outt{i}", [128, D], F32) for i in range(2)]
    ssq = sbB("ssq", [128, 2], F32)
    lssq = sbB("lssq", [128, 2], F32)
    sqs = sbB("sqs", [128, 2], F32)
    dens = sbB("dens", [128, 2], F32)
    scaleT = [sbB(f"scaleT{i}", [128, 2], F32) for i in range(2)]
    pool2 = [sbB(f"pool2_{i}", [128, 16], F32) for i in range(2)]
    stt2 = [sbB(f"stt2_{i}", [128, 12], F32) for i in range(2)]

    psB = contextlib.ExitStack()
    arena = [psB.enter_context(nc.psum_tensor(f"arena{i}", [128, 1024], F32)) for i in range(2)]
    povT = psB.enter_context(nc.psum_tensor("povT", [128, 512], F32))
    spare = psB.enter_context(nc.psum_tensor("spare", [128, 512], F32))
    ph = [psB.enter_context(nc.psum_tensor(f"ph{i}", [128, 512], F32)) for i in range(2)]

    bank = {"spare": None, "povT": None}
    arena_free = [None, None]
    busers = {"ph0": None, "ph1": None, "spare": None, "povT": None}
    bmap = {"ph0": ph[0], "ph1": ph[1], "spare": spare, "povT": povT}
    util_ptr = [0]
    up_ptr = [0]
    dummy_tok = [None]

    def util_next():
        k = ("spare", "ph1")[util_ptr[0]]
        util_ptr[0] ^= 1
        return k

    def up_next():
        k = ("ph0", "ph1", "povT")[up_ptr[0]]
        up_ptr[0] = (up_ptr[0] + 1) % 3
        return k

    NG2 = NK // 2

    ws = {"load": [None] * 64, "tern": [None] * 64, "write": [None] * 64,
          "win": [wred_tok[62], wred_tok[63]], "wtu": [None] * 2, "tmpu": None}

    def chunk_src(i):
        if i < 32:
            return wup_h[i * 128:(i + 1) * 128, :]
        j = i - 32
        return wdn_h[(j // 4) * 128:(j // 4 + 1) * 128, (j % 4) * D:(j % 4 + 1) * D]

    def chunk_dst(i):
        if i < 32:
            return wups[i * 128:(i + 1) * 128, :]
        j = i - 32
        return wdns[(j // 4) * 128:(j // 4 + 1) * 128, (j % 4) * D:(j % 4 + 1) * D]

    def mk_p2_load(i):
        def f():
            ws["load"][i] = b.dma("sync", w_in[i % 2][:], chunk_src(i), ws["win"][i % 2])
        return f

    def mk_p2_tern(i):
        def f():
            col = 0 if i < 32 else 1
            t1 = b.v(lambda e, col=col: e.tensor_scalar(
                wt_tmp[:], w_in[i % 2][:], tau_n[:, col:col + 1], None, ALU.is_lt),
                ws["load"][i], fin_tok, ws["tmpu"], ws["wtu"][i % 2])
            t2 = b.v(lambda e, col=col: e.scalar_tensor_tensor(
                wt_bf[i % 2][:], w_in[i % 2][:], tau_p[:, col:col + 1], wt_tmp[:],
                ALU.is_gt, ALU.subtract), t1)
            ws["win"][i % 2] = t2
            ws["tmpu"] = t2
            g1 = b.dma("gpsimd", chunk_dst(i), wt_bf[i % 2][:], t2)
            ws["write"][i] = g1
            ws["wtu"][i % 2] = g1
        return f

    d_wuT = [None] * 8

    def mk_read_up(dt):
        def f():
            d_wuT[dt] = b.dma("sync", wuT[dt][:], wups[0:DFF, dt * 128:(dt + 1) * 128],
                              *ws["write"][0:32], transpose=True)
        return f

    wtasks = []
    for i in range(64):
        if i >= 2:
            wtasks.append(mk_p2_tern(i - 2))
        wtasks.append(mk_p2_load(i))
    for i in range(62, 64):
        wtasks.append(mk_p2_tern(i))
    for dt in range(8):
        wtasks.append(mk_read_up(dt))
    wti = [0]

    def pump_wtasks(n):
        k = 0
        while wti[0] < len(wtasks) and k < n:
            wtasks[wti[0]]()
            wti[0] += 1
            k += 1

    d_xnT_toks = [None, None]
    xnT_user = [None, None]
    hT_user = [None, None]
    hev = {}
    gelu_tok = {}
    g2_user = [None] * 2
    sq_last = {}
    ss_last = {}
    scaleT_tok = {}
    scaleT_user = [None, None]
    ring_user = [None] * NRING
    ring_tok = {}
    x2i_user = [None, None]
    outt_user = [None, None]
    p2pool_user = [None, None]
    p2stt_user = [None, None]
    g_out = []

    def mk_xnt_reads(tb):
        def f():
            par = (tb // 1) % 2
            toks = []
            for dt in range(8):
                toks.append(b.dma(
                    "sync",
                    xnT_in[par][:, dt * 256:(dt + 1) * 256],
                    xns[tb * 256:(tb + 1) * 256, dt * 128:(dt + 1) * 128],
                    g_xn[tb * 2], g_xn[tb * 2 + 1],
                    xnT_user[par] if dt == 0 else None, transpose=True))
            d_xnT_toks[par] = toks
        return f

    def mk_up(tb, ft):
        def f():
            par = tb % 2
            pk = up_next()
            pb = bmap[pk]
            mm = None
            for dt in range(8):
                mm = b.t(lambda e, dt=dt, ft=ft, par=par, pb=pb: e.matmul(
                    pb[:, 0:256], wuT[dt][:, ft * 128:(ft + 1) * 128],
                    xnT_in[par][:, dt * 256:(dt + 1) * 256],
                    start=(dt == 0), stop=(dt == 7), skip_group_check=True),
                    d_xnT_toks[par][dt] if ft == 0 else None,
                    d_wuT[dt],
                    busers[pk] if dt == 0 else None)
            ge = b.s(lambda e, ft=ft, par=par, pb=pb: e.activation(
                hT_sb[par][:, ft * 256:(ft + 1) * 256], pb[:, 0:256], AF.Gelu),
                mm, hT_user[par] if ft == 0 else None)
            busers[pk] = ge
            hev[(tb, ft)] = ge
            if ft == 31:
                xnT_user[par] = mm
        return f

    def mk_sqss(tb, ft):
        def f():
            par = tb % 2
            sq = b.v(lambda e, ft=ft, par=par: e.tensor_mul(
                g2[ft % 2][:], hT_sb[par][:, ft * 256:(ft + 1) * 256],
                hT_sb[par][:, ft * 256:(ft + 1) * 256]),
                hev[(tb, ft)], g2_user[ft % 2])
            tok = None
            for ts2 in range(2):
                tok = b.t(lambda e, ft=ft, ts2=ts2: e.matmul(
                    spare[:, ts2:ts2 + 1], g2[ft % 2][:, ts2 * 128:(ts2 + 1) * 128],
                    ones_col[:], start=(ft == 0 and ts2 == 0),
                    stop=(ft == 31 and ts2 == 1), skip_group_check=True),
                    sq, t_ones_col,
                    busers["spare"] if (ft == 0 and ts2 == 0) else None)
            g2_user[ft % 2] = tok
            ss_last[tb] = tok
        return f

    def mk_rms(tb, idx):
        def f():
            c1 = b.v(lambda e: e.tensor_copy(ssq[:], spare[:, 0:2]), ss_last[tb],
                     scaleT_user[idx])
            bank["spare"] = c1
            busers["spare"] = c1
            c2 = b.s(lambda e: e.activation(lssq[:], ssq[:], AF.Ln), c1)
            c3 = b.s(lambda e: e.activation(sqs[:], lssq[:], AF.Exp, scale=0.5), c2)
            c4 = b.v(lambda e: e.tensor_scalar(dens[:], sqs[:], 1.0 / 64.0, RMS_EPS,
                                               ALU.mult, ALU.add), c3)
            c5 = b.v(lambda e, idx=idx: e.reciprocal(scaleT[idx][:], dens[:]), c4)
            scaleT_tok[tb] = c5
        return f

    dp_state = {}

    def mk_down_mms(ta, tb_):
        def f():
            accs = [ph[0][:], ph[1][:], povT[:], spare[:],
                    arena[0][:, 0:512], arena[0][:, 512:1024],
                    arena[1][:, 0:512], arena[1][:, 512:1024]]
            acc_w = [busers["ph0"], busers["ph1"], busers["povT"], busers["spare"],
                     arena_free[0], arena_free[0], arena_free[1], arena_free[1]]
            ring_tok = [None] * 32

            def ring_read(ft):
                r = ft % NRING
                ring_tok[ft] = b.dma("sync", wdR[r][:],
                                     wdns[0:D, ft * 128:(ft + 1) * 128],
                                     *ws["write"][32:64], ring_user[r], transpose=True)

            for ft in range(NRING):
                ring_read(ft)
            mmj = [None] * 8
            for ft in range(32):
                for j in range(8):
                    tb = (ta, tb_)[j // 4]
                    par = tb % 2
                    tsub = (j // 2) % 2
                    ch = j % 2
                    mmj[j] = b.t(lambda e, ft=ft, j=j, par=par, tsub=tsub, ch=ch: e.matmul(
                        accs[j],
                        hT_sb[par][:, ft * 256 + tsub * 128:ft * 256 + (tsub + 1) * 128],
                        wdR[ft % NRING][:, ch * 512:(ch + 1) * 512],
                        start=(ft == 0), stop=(ft == 31), skip_group_check=True),
                        hev[(tb, ft)] if j % 4 == 0 else None,
                        ring_tok[ft] if j == 0 else None,
                        acc_w[j] if ft == 0 else None)
                ring_user[ft % NRING] = mmj[7]
                if ft + NRING < 32:
                    ring_read(ft + NRING)
            hT_user[ta % 2] = mmj[3]
            hT_user[tb_ % 2] = mmj[7]
            dp_state["mmj7"] = mmj[7]
            dp_state["accs"] = accs
            dp_state["evs"] = [None] * 8
            dp_state["d_x2"] = [None] * 4
        return f

    def mk_tail_evac(ta, tb_, q):
        accs, evs, d_x2 = dp_state["accs"], dp_state["evs"], dp_state["d_x2"]
        tb = (ta, tb_)[q // 2]
        tsub = q % 2
        idx = q // 2
        d_x2[q] = b.dma("gpsimd", x2_in[q % 2][:],
                        x2s[tb * 256 + tsub * 128:tb * 256 + (tsub + 1) * 128, :],
                        g_x2[tb * 2 + tsub], x2i_user[q % 2])
        for ch in range(2):
            j = q * 2 + ch
            evs[j] = b.v(lambda e, q=q, ch=ch, j=j, idx=idx, tsub=tsub, accs=accs: e.tensor_scalar(
                outt[q % 2][:, ch * 512:(ch + 1) * 512], accs[j],
                scaleT[idx][:, tsub:tsub + 1], None, ALU.mult),
                dp_state["mmj7"], scaleT_tok[tb], outt_user[q % 2] if ch == 0 else None)
        scaleT_user[idx] = evs[q * 2 + 1]

    def mk_tail_chain(ta, tb_, q):
        evs, d_x2 = dp_state["evs"], dp_state["d_x2"]
        tb = (ta, tb_)[q // 2]
        tsub = q % 2
        gt = tb * 2 + tsub
        p, st = pool2[q % 2], stt2[q % 2]
        v1 = b.v(lambda e, st=st, q=q: e.bn_stats(
            st[:, 0:6], outt[q % 2][:, 0:512]), evs[q * 2], p2stt_user[q % 2])
        v2 = b.v(lambda e, st=st, q=q: e.bn_stats(
            st[:, 6:12], outt[q % 2][:, 512:1024]), evs[q * 2 + 1])
        v3 = b.v(lambda e, st=st, p=p: e.bn_aggr(p[:, 0:2], st[:]), v1, v2,
                 p2pool_user[q % 2])
        p2stt_user[q % 2] = v3
        v4 = b.v(lambda e, p=p: e.tensor_scalar(p[:, 2:3], p[:, 1:2], LN_EPS,
                                                None, ALU.add), v3)
        s1 = b.s(lambda e, p=p: e.activation(p[:, 3:4], p[:, 2:3], AF.Ln), v4)
        s2 = b.s(lambda e, p=p: e.activation(p[:, 4:5], p[:, 3:4], AF.Exp,
                                             scale=-0.5), s1)
        v5 = b.v(lambda e, p=p: e.tensor_scalar(p[:, 5:6], p[:, 0:1], p[:, 4:5],
                                                -1.0, ALU.mult, ALU.mult), s2)
        v6 = b.v(lambda e, q=q, p=p: e.tensor_scalar(
            outt[q % 2][:], outt[q % 2][:], p[:, 0:1], p[:, 4:5],
            ALU.subtract, ALU.mult), v5)
        p2pool_user[q % 2] = v6
        v7 = b.v(lambda e, q=q: e.tensor_tensor(
            outt[q % 2][:], outt[q % 2][:], x2_in[q % 2][:], ALU.add),
            v6, d_x2[q])
        x2i_user[q % 2] = v7
        gdo = b.dma("gpsimd", out_h[gt * 128:(gt + 1) * 128, :], outt[q % 2][:], v7)
        outt_user[q % 2] = gdo
        g_out.append(gdo)

    def mk_tails(ta, tb_):
        def f_evac01():
            mk_tail_evac(ta, tb_, 0)
            mk_tail_evac(ta, tb_, 1)
            evs = dp_state["evs"]
            busers["ph0"] = evs[0]
            busers["ph1"] = evs[1]
            busers["povT"] = evs[2]
            bank["povT"] = evs[2]
            busers["spare"] = evs[3]
            bank["spare"] = evs[3]

        def f_chain01():
            mk_tail_chain(ta, tb_, 0)
            mk_tail_chain(ta, tb_, 1)

        def f_q23():
            evs = dp_state["evs"]
            mk_tail_evac(ta, tb_, 2)
            mk_tail_chain(ta, tb_, 2)
            mk_tail_evac(ta, tb_, 3)
            mk_tail_chain(ta, tb_, 3)
            arena_free[0] = evs[5]
            arena_free[1] = evs[7]
        return (f_evac01, f_chain01, f_q23)

    def ffn_work(tbs):
        work = []
        ta, tb_ = tbs
        work.append(("fine", mk_xnt_reads(ta)))
        work.append(("slot0", None))
        for ft in range(32):
            work.append(("fine", mk_up(ta, ft)))
            if ft == 8:
                work.append(("slot1", None))
            if ft == 16:
                work.append(("fine", mk_xnt_reads(tb_)))
            if ft == 21:
                work.append(("slot2", None))
        for ft in range(32):
            work.append(("fine", mk_sqss(ta, ft)))
        work.append(("fine", mk_rms(ta, 0)))
        for ft in range(32):
            work.append(("fine", mk_up(tb_, ft)))
            if ft >= 2:
                work.append(("fine", mk_sqss(tb_, ft - 2)))
        work.append(("fine", mk_sqss(tb_, 30)))
        work.append(("fine", mk_sqss(tb_, 31)))
        work.append(("fine", mk_rms(tb_, 1)))
        work.append(("macro", mk_down_mms(ta, tb_)))
        return work

    arena_exp = arena_free
    pT_user = [None, None]
    ovst_user = [None, None]
    ovtn_user = [[None, None], [None, None]]
    attn_user = AliasTracker()
    xn_user = AliasTracker()
    xload = [None] * NT
    g_x2 = [None] * NT
    g_xn = [None] * NT
    xsb_user = [[s_ln1[NT - 2]], [s_ln1[NT - 1]]]
    pool_user = [s_ln1[NT - 2], s_ln1[NT - 1]]

    work = []
    wk = [0]

    def fines_left():
        n = 0
        for i in range(wk[0], len(work)):
            if work[i][0] == "macro":
                n += 0
            else:
                n += 1
        return n

    def pump_fine(n):
        k = 0
        while wk[0] < len(work) and k < n:
            kind, fn = work[wk[0]]
            if kind == "macro":
                break
            fn()
            wk[0] += 1
            k += 1

    def pump_boundary():
        if wk[0] < len(work) and work[wk[0]][0] == "macro":
            work[wk[0]][1]()
            wk[0] += 1

    def pump_all():
        while wk[0] < len(work):
            work[wk[0]][1]()
            wk[0] += 1

    def mk_dummy():
        tok = b.t(lambda e: e.matmul(
            ph[0][:], ckv1[:, 0:128], khT[:, 0:512],
            start=True, stop=True, skip_group_check=True),
            t_khT_ev[0], d_ckv[0], t_khT_z, dummy_tok[0], busers["ph0"])
        dummy_tok[0] = tok
        busers["ph0"] = tok

    def mk_qb_tail(qb, ovn_gds):
        items = []
        qpar = qb % 2

        def mk_itemA(tt):
            def f():
                gt = qb * 4 + tt
                tsl = slice(tt * 128, (tt + 1) * 128)
                p, st = pool[tt % 2], stt[tt % 2]
                pw = pool_user[tt % 2]
                pks = [None, None]
                mms = [None, None]
                for ch in range(2):
                    csl = slice(ch * 512, (ch + 1) * 512)
                    pk = util_next()
                    pb = bmap[pk]
                    pks[ch] = pk
                    b.t(lambda e, tsl=tsl, csl=csl, pb=pb: e.matmul(
                        pb[:], ovtn[qpar][0][:, tsl], wc_pack[0][:, csl],
                        start=True, stop=False, skip_group_check=True),
                        *ovn_gds, *g_wc, busers[pk])
                    b.t(lambda e, tsl=tsl, csl=csl, pb=pb: e.matmul(
                        pb[:], ovtn[qpar][1][:, tsl], wc_pack[1][:, csl],
                        start=False, stop=False, skip_group_check=True))
                    mms[ch] = b.t(lambda e, csl=csl, pb=pb: e.matmul(
                        pb[:], ones_row[0:1, :], beff[0:1, csl],
                        start=False, stop=True, skip_group_check=True), t_beff_bf)
                    ovtn_user[qpar][0] = mms[ch]
                    ovtn_user[qpar][1] = mms[ch]
                v1 = b.v(lambda e, st=st: e.bn_stats(st[:, 0:6], bmap[pks[0]][:]),
                         mms[0], pw)
                v2 = b.v(lambda e, st=st: e.bn_stats(st[:, 6:12], bmap[pks[1]][:]), mms[1])
                v3 = b.v(lambda e, st=st, p=p: e.bn_aggr(p[:, 0:2], st[:]), v1, v2)
                v4 = b.v(lambda e, p=p: e.tensor_scalar(p[:, 2:3], p[:, 1:2], LN_EPS, None,
                                                        ALU.add), v3)
                s1 = b.s(lambda e, p=p: e.activation(p[:, 3:4], p[:, 2:3], AF.Ln), v4)
                s2 = b.s(lambda e, p=p: e.activation(p[:, 4:5], p[:, 3:4], AF.Exp,
                                                     scale=-0.5), s1)
                v5 = b.v(lambda e, p=p: e.tensor_scalar(p[:, 5:6], p[:, 0:1], p[:, 4:5],
                                                        -1.0, ALU.mult, ALU.mult), s2)
                ap0 = b.v(lambda e, tt=tt, p=p: e.tensor_scalar(
                    attn_sb[tt % 2][:, 0:512], bmap[pks[0]][:], p[:, 0:1], p[:, 4:5],
                    ALU.subtract, ALU.mult), v5, attn_user[tt % 2])
                ap1 = b.v(lambda e, tt=tt, p=p: e.tensor_scalar(
                    attn_sb[tt % 2][:, 512:1024], bmap[pks[1]][:], p[:, 0:1], p[:, 4:5],
                    ALU.subtract, ALU.mult), v5)
                busers[pks[0]] = ap0
                busers[pks[1]] = ap1
                v7 = b.v(lambda e, tt=tt, gt=gt: e.tensor_tensor(
                    x_sb[gt % 2][:], attn_sb[tt % 2][:], x_sb[gt % 2][:], ALU.add),
                    ap0, ap1, xload[gt])
                attn_user[tt % 2] = v7
                g_x2[gt] = b.dma("gpsimd", x2s[gt * 128:(gt + 1) * 128, :],
                                 x_sb[gt % 2][:], v7)
                tailst["v7"][tt] = v7
                tailst["v3"][tt] = v3
            return f

        def mk_itemB(tt):
            def f():
                gt = qb * 4 + tt
                p, st = pool[tt % 2], stt[tt % 2]
                v7 = tailst["v7"][tt]
                u1 = b.v(lambda e, st=st, gt=gt: e.bn_stats(st[:, 0:6],
                                                            x_sb[gt % 2][:, 0:512]),
                         v7, tailst["v3"][tt])
                u2 = b.v(lambda e, st=st, gt=gt: e.bn_stats(st[:, 6:12],
                                                            x_sb[gt % 2][:, 512:1024]), v7)
                u3 = b.v(lambda e, st=st, p=p: e.bn_aggr(p[:, 6:8], st[:]), u1, u2)
                u4 = b.v(lambda e, p=p: e.tensor_scalar(p[:, 8:9], p[:, 7:8], LN_EPS, None,
                                                        ALU.add), u3)
                w1 = b.s(lambda e, p=p: e.activation(p[:, 9:11], p[:, 7:9], AF.Ln), u4)
                w2 = b.s(lambda e, p=p: e.activation(p[:, 11:13], p[:, 9:11], AF.Exp,
                                                     scale=0.5), w1)
                u5 = b.v(lambda e, p=p: e.scalar_tensor_tensor(p[:, 13:14], p[:, 12:13],
                                                               RMS_EPS, p[:, 11:12],
                                                               ALU.mult, ALU.add), w2)
                u6 = b.v(lambda e, p=p: e.reciprocal(p[:, 14:15], p[:, 13:14]), u5)
                u7 = b.v(lambda e, p=p: e.tensor_scalar(p[:, 15:16], p[:, 6:7], p[:, 14:15],
                                                        -1.0, ALU.mult, ALU.mult), u6)
                sxn = b.v(lambda e, tt=tt, gt=gt, p=p: e.tensor_scalar(
                    xn_bf[tt % 2][:], x_sb[gt % 2][:], p[:, 6:7], p[:, 14:15],
                    ALU.subtract, ALU.mult), u7, xn_user[tt % 2])
                pool_user[tt % 2] = u7
                g_xn[gt] = b.dma("gpsimd", xns[gt * 128:(gt + 1) * 128, :],
                                 xn_bf[tt % 2][:], sxn)
                xn_user[tt % 2] = g_xn[gt]
                xsb_user[gt % 2] = [sxn, g_x2[gt]]
                g2l = gt + 2
                if g2l < NT:
                    xload[g2l] = b.dma("sync", x_sb[g2l % 2][:],
                                       x_h[g2l * 128:(g2l + 1) * 128, :],
                                       *xsb_user[g2l % 2])
            return f

        for tt in range(4):
            items.append(("fine", mk_itemA(tt)))
            items.append(("fine", mk_itemB(tt)))
        return items


    def weave(tail_items, ffn_items):
        if not tail_items:
            return ffn_items
        cut = 34
        return (tail_items[0:4] + ffn_items[0:1]
                + tail_items[4:8] + ffn_items[1:cut]
                + ffn_items[cut:])

    tailst = {"v7": [None] * 4, "v3": [None] * 4}
    pending_tail = []

    for qb in range(NQ):
        assert wk[0] == len(work), f"ffn backlog {len(work)-wk[0]} at qb{qb}"
        work = pending_tail
        pending_tail = []
        wk[0] = 0
        qsl = slice(qb * 512, (qb + 1) * 512)
        if qb == 0:
            for tt in range(2):
                xload[tt] = b.dma("sync", x_sb[tt % 2][:], x_h[tt * 128:(tt + 1) * 128, :],
                                  *xsb_user[tt % 2])
        qh_ev = [None] * H
        for h in range(H):
            pk = util_next()
            pb = bmap[pk]
            mmq = b.t(lambda e, h=h, pb=pb, qsl=qsl: e.matmul(
                pb[0:64, :], lwabs[:, h * 64:(h + 1) * 64], cqT[:, qsl],
                start=True, stop=True, skip_group_check=True),
                t_cqT_ev[qb], busers[pk], t_lwabs[-1])
            qh_ev[h] = b.v(lambda e, h=h, pb=pb: e.tensor_scalar(
                qhT[h][0:64, :], pb[0:64, :], bias64[:, h:h + 1], None, ALU.add),
                mmq, last_pmB, d_bqr, t_qhT_z[h])
            busers[pk] = qh_ev[h]

        pend_tail = None
        ovn_gds = []
        for h in range(H):
            sc_tok = [None] * NG2
            e_tok = [None] * NG2

            def emit_S(g, h=h, qb=qb):
                tok = None
                for j in range(2):
                    kt = 2 * g + j
                    tok = b.t(lambda e, kt=kt, j=j, g=g, h=h: e.matmul(
                        arena[g % 2][:, j * 512:(j + 1) * 512],
                        khT[:, kt * 128:(kt + 1) * 128], qhT[h][:],
                        start=True, stop=True, skip_group_check=True),
                        qh_ev[h] if j == 0 else None,
                        t_khT_ev[-1] if (h == 0 and qb == 0 and j == 0) else None,
                        arena_free[g % 2] if j == 0 else None)
                sc_tok[g] = tok

            def emit_E(g, h=h):
                e_tok[g] = b.s(lambda e, g=g: e.activation(
                    pT[g % 2][:], arena[g % 2][:], AF.Exp, scale=SCALE),
                    sc_tok[g], pT_user[g % 2])
                arena_free[g % 2] = e_tok[g]

            def emit_O(g, h=h):
                tok = None
                for j in range(2):
                    kt = 2 * g + j
                    tok = b.t(lambda e, kt=kt, j=j, g=g: e.matmul(
                        povT[:], ckv1[:, kt * 128:(kt + 1) * 128],
                        pT[g % 2][:, j * 512:(j + 1) * 512],
                        start=(kt == 0), stop=(kt == NK - 1), skip_group_check=True),
                        e_tok[g] if j == 0 else None,
                        *(ckv_w if kt == 0 else []),
                        bank["povT"] if kt == 0 else None)
                pT_user[g % 2] = tok
                return tok

            emit_S(0)
            emit_E(0)
            if pend_tail is not None:
                pend_tail()
                pend_tail = None
            if qb == 0:
                pump_wtasks(-(-(len(wtasks) - wti[0]) // (H - h)))
            mk_dummy()
            for g in range(1, NG2):
                emit_S(g)
                emit_E(g)
                mk_dummy()
                emit_O(g - 1)
            last_ov = emit_O(NG2 - 1)

            def mk_tail(h=h, last_ov=last_ov, qb=qb):
                def f():
                    rll = b.s(lambda e: e.activation(rl_f[:], povT[32:33, :], AF.Ln),
                              last_ov)
                    rle = b.s(lambda e: e.activation(rl_bf[:], rl_f[:], AF.Exp,
                                                     scale=-1.0), rll)
                    pk = util_next()
                    pb = bmap[pk]
                    rb = b.t(lambda e, pb=pb: e.matmul(
                        pb[0:32, :], ones_row[0:1, 0:32], rl_bf[:],
                        start=True, stop=True, skip_group_check=True),
                        rle, t_ones_row, busers[pk])
                    rbe = b.v(lambda e, pb=pb: e.tensor_copy(rb_sb[:], pb[0:32, :]),
                              rb)
                    busers[pk] = rbe
                    ovn = b.v(lambda e, h=h: e.tensor_tensor(
                        ovst[h % 2][:], povT[0:32, :], rb_sb[:], ALU.mult),
                        rbe, ovst_user[h % 2])
                    bank["povT"] = ovn
                    gd = b.dma("gpsimd", ovtn[qb % 2][h // 4][(h % 4) * 32:(h % 4 + 1) * 32, :],
                               ovst[h % 2][:], ovn, ovtn_user[qb % 2][h // 4])
                    ovst_user[h % 2] = gd
                    ovn_gds.append(gd)
                    return gd
                return f

            pend_tail = mk_tail()
            if qb != 0:
                pump_fine(1)
        pend_tail()
        if qb == 0:
            while wti[0] < len(wtasks):
                pump_wtasks(8)
        else:
            pump_all()
        pending_tail = mk_qb_tail(qb, ovn_gds)



    busers["povT"] = bank["povT"]
    pairs = []
    prev_tails = None
    for j in range(NQ):
        w = ffn_work((2 * j, 2 * j + 1))
        w2 = []
        for kind, fn in w:
            if kind.startswith("slot"):
                if prev_tails is not None:
                    w2.append(("fine", prev_tails[int(kind[4])]))
            else:
                w2.append((kind, fn))
        prev_tails = mk_tails(2 * j, 2 * j + 1)
        pairs.append(w2)
    for t in prev_tails:
        pairs[-1].append(("fine", t))
    for j in range(NQ - 1):
        nxt = pairs[j + 1].pop(0)
        pairs[j].insert(len(pairs[j]) - 1, nxt)
    work = list(pending_tail)
    for pr in pairs:
        work += pr
    wk[0] = 0
    pump_all()

    gfinal = [(b.sem[k], b.c[k]) for k in
              [f"g{i}" for i in range(4)] + [f"e{i}" for i in range(4)]]

    def _final_wait(e, gfinal=gfinal):
        for smh, val in gfinal:
            e.wait_ge(smh, val)
    b.raw("gpsimd", _final_wait)

    with nc.allow_low_precision(reason="bf16 matmul operands per problem tolerance spec"):
        with nc.Block() as block:
            @block.tensor
            def _(eng):
                for f in b.q["tensor"]:
                    f(eng)

            @block.vector
            def _(eng):
                for f in b.q["vector"]:
                    f(eng)

            @block.scalar
            def _(eng):
                for f in b.q["scalar"]:
                    f(eng)

            @block.sync
            def _(eng):
                for f in b.q["sync"]:
                    f(eng)

            @block.gpsimd
            def _(eng):
                for f in b.q["gpsimd"]:
                    f(eng)

    psB.close()
    scB.close()
    stack.close()
    return nc


def kernel(**inputs):
    x = np.ascontiguousarray(np.asarray(inputs["L0_attn_h_t"], dtype=np.float32))
    B_, S, D_ = x.shape
    wmap = {
        "W_DQ": "L0_attn_W_DQ", "b_DQ": "L0_attn_b_DQ",
        "W_UQ": "L0_attn_W_UQ", "b_UQ": "L0_attn_b_UQ",
        "W_DKV": "L0_attn_W_DKV", "b_DKV": "L0_attn_b_DKV",
        "W_UK": "L0_attn_W_UK",
        "W_UV": "L0_attn_W_UV", "b_UV": "L0_attn_b_UV",
        "W_QR": "L0_attn_W_QR", "b_QR": "L0_attn_b_QR",
        "W_KR": "L0_attn_W_KR", "b_KR": "L0_attn_b_KR",
        "W_O": "L0_attn_W_O", "b_O": "L0_attn_b_O",
        "w_up": "L0_ffn_w_up", "w_down": "L0_ffn_w_down",
    }
    wd = {k: np.ascontiguousarray(np.asarray(inputs[v], dtype=np.float32))
          for k, v in wmap.items()}
    nc = build_nc(S)
    in_maps = [dict(x=x[i], **wd) for i in range(B_)]
    res = run_bass_kernel_spmd(nc, in_maps, core_ids=list(range(B_)))
    out = np.stack([np.asarray(r["out"], dtype=np.float32) for r in res.results], axis=0)
    return out


# version 25 (final)
# speedup vs baseline: 1.0423x; 1.0326x over previous
import numpy as np
import contextlib
import concourse.bass as bass
import concourse.mybir as mybir
from concourse.bass_utils import run_bass_kernel_spmd

F32 = mybir.dt.float32
BF16 = mybir.dt.bfloat16
AF = mybir.ActivationFunctionType
ALU = mybir.AluOpType
AX = mybir.AxisListType

D = 1024
H = 8
DFF = 4096
SCALE = float(1.0 / np.sqrt(160.0))
LN_EPS = 1e-5
RMS_EPS = 1e-8
N_CORES = 8


class AliasTracker(list):

    def __init__(self):
        super().__init__([None])

    def __getitem__(self, i):
        return super().__getitem__(0)

    def __setitem__(self, i, v):
        super().__setitem__(0, v)


class Builder:
    ENGS = ("tensor", "vector", "scalar", "sync", "gpsimd")
    E2S = {"tensor": "t", "vector": "v", "scalar": "s", "sync": "d", "gpsimd": "g"}

    DK = {"sync": 8, "gpsimd": 4, "scalar": 4}

    def __init__(self, nc, stack):
        self.nc = nc
        self.q = {e: [] for e in self.ENGS}
        keys = (["t", "v", "s"] + [f"d{i}" for i in range(8)]
                + [f"g{i}" for i in range(4)] + [f"e{i}" for i in range(4)])
        self.c = {k: 0 for k in keys}
        self.sem = {k: stack.enter_context(nc.semaphore("sem_" + k)) for k in keys}
        self.hw = {e: {k: 0 for k in keys} for e in self.ENGS}
        self.pending = {e: [] for e in self.ENGS}
        self.dptr = {"sync": 0, "gpsimd": 0, "scalar": 0}

    def barrier(self):
        snap = [(k, self.c[k]) for k in self.c]
        for e in self.ENGS:
            self.pending[e] = list(snap)

    def _emit(self, eng, fn, waits, semkey=None):
        hw = self.hw[eng]
        wl = []
        allw = self.pending[eng] + [w for w in waits if w is not None]
        self.pending[eng] = []
        for k, val in allw:
            assert val <= self.c[k], f"wait on future {k}:{val} > {self.c[k]}"
            if val > hw[k]:
                hw[k] = val
                wl.append((self.sem[k], val))
        sk = semkey if semkey is not None else self.E2S[eng]
        inc = 16 if (len(sk) > 1 and sk[0] in ("d", "g", "e")) else 1
        self.c[sk] += inc
        tok = self.c[sk]
        sem = self.sem[sk]

        def run(e, fn=fn, wl=wl, sem=sem, inc=inc):
            for (h, v) in wl:
                e.wait_ge(h, v)
            fn(e).then_inc(sem, inc)

        self.q[eng].append(run)
        return (sk, tok)

    def t(self, fn, *waits):
        return self._emit("tensor", fn, waits)

    def v(self, fn, *waits):
        return self._emit("vector", fn, waits)

    def s(self, fn, *waits):
        return self._emit("scalar", fn, waits)

    def dma(self, eng, out, in_, *waits, transpose=False, slow=False):
        def fn(e, out=out, in_=in_, transpose=transpose, slow=slow):
            kw = {}
            if transpose:
                kw["transpose"] = True
            if slow:
                kw["allow_slow_non_contiguous"] = True
            return e.dma_start(out=out, in_=in_, **kw)
        K = self.DK[eng]
        ptr = self.dptr[eng]
        self.dptr[eng] = (ptr + 1) % K
        key = {"sync": "d", "gpsimd": "g", "scalar": "e"}[eng] + str(ptr)
        chain = (key, self.c[key])
        return self._emit(eng, fn, list(waits) + [chain], semkey=key)

    def fence(self, eng):
        pre = "d" if eng == "sync" else "g"
        K = self.DK[eng]
        return [(pre + str(i), self.c[pre + str(i)]) for i in range(K)]

    def raw(self, eng, fn):
        self.q[eng].append(fn)


def build_nc(S):
    NT = S // 128
    NQ = S // 512
    NK = S // 128
    TB = S // 256
    NG = -(-NK // 3)
    nc = bass.Bass("TRN2", target_bir_lowering=False)

    x_h = nc.dram_tensor("x", [S, D], F32, kind="ExternalInput").ap()
    out_h = nc.dram_tensor("out", [S, D], F32, kind="ExternalOutput").ap()
    wdq_h = nc.dram_tensor("W_DQ", [D, 32], F32, kind="ExternalInput").ap()
    bdq_h = nc.dram_tensor("b_DQ", [32], F32, kind="ExternalInput").ap()
    wuq_h = nc.dram_tensor("W_UQ", [32, D], F32, kind="ExternalInput").ap()
    buq_h = nc.dram_tensor("b_UQ", [D], F32, kind="ExternalInput").ap()
    wdkv_h = nc.dram_tensor("W_DKV", [D, 32], F32, kind="ExternalInput").ap()
    bdkv_h = nc.dram_tensor("b_DKV", [32], F32, kind="ExternalInput").ap()
    wuk_h = nc.dram_tensor("W_UK", [32, D], F32, kind="ExternalInput").ap()
    wuv_h = nc.dram_tensor("W_UV", [32, D], F32, kind="ExternalInput").ap()
    buv_h = nc.dram_tensor("b_UV", [D], F32, kind="ExternalInput").ap()
    wqr_h = nc.dram_tensor("W_QR", [32, 256], F32, kind="ExternalInput").ap()
    bqr_h = nc.dram_tensor("b_QR", [256], F32, kind="ExternalInput").ap()
    wkr_h = nc.dram_tensor("W_KR", [D, 32], F32, kind="ExternalInput").ap()
    bkr_h = nc.dram_tensor("b_KR", [32], F32, kind="ExternalInput").ap()
    wo_h = nc.dram_tensor("W_O", [D, D], F32, kind="ExternalInput").ap()
    bo_h = nc.dram_tensor("b_O", [D], F32, kind="ExternalInput").ap()
    wup_h = nc.dram_tensor("w_up", [DFF, D], F32, kind="ExternalInput").ap()
    wdn_h = nc.dram_tensor("w_down", [D, DFF], F32, kind="ExternalInput").ap()
    ln1s = nc.dram_tensor("ln1s", [S, D], BF16, kind="Internal").ap()
    xns = nc.dram_tensor("xns", [S, D], BF16, kind="Internal").ap()
    x2s = nc.dram_tensor("x2s", [S, D], F32, kind="Internal").ap()
    wups = nc.dram_tensor("wups", [DFF, D], BF16, kind="Internal").ap()
    wdns = nc.dram_tensor("wdns", [D, DFF], BF16, kind="Internal").ap()

    stack = contextlib.ExitStack()
    b = Builder(nc, stack)
    sb = lambda n, shp, dt=BF16: stack.enter_context(nc.sbuf_tensor(n, shp, dt))

    ones_row = sb("ones_row", [1, 128], BF16)
    ones_col = sb("ones_col", [128, 1], BF16)
    onesf_row = sb("onesf_row", [1, 128], F32)
    accw = sb("accw", [128, 1], F32)
    bkh = sb("bkh", [64, 1], F32)
    bdq = sb("bdq", [32, 1], F32)
    bias64 = sb("bias64", [64, 8], F32)
    beff = sb("beff", [1, D], BF16)
    wc_pack = [sb(f"wc{j}", [128, D], BF16) for j in range(2)]
    lwabs = sb("lwabs", [128, 512], BF16)
    cqT = sb("cqT", [128, S], BF16)
    khT = sb("khT", [128, S], BF16)
    ckv1 = sb("ckv1", [128, 128 * NK], BF16)
    tau_p = sb("tau_p", [128, 2], F32)
    tau_n = sb("tau_n", [128, 2], F32)
    acc = sb("acc", [128, 2], F32)
    x_sb = [sb(f"x_sb{i}", [128, D], F32) for i in range(2)]
    pool = [sb(f"pool{i}", [128, 16], F32) for i in range(2)]
    stt = [sb(f"stt{i}", [128, 12], F32) for i in range(2)]

    w_in = [sb(f"w_in{i}", [128, D], F32) for i in range(2)]
    wt_bf = [sb(f"wt_bf{i}", [128, D], BF16) for i in range(2)]
    wt_tmp = sb("wt_tmp", [128, D], BF16)
    wred = sb("wred", [128, 1], F32)
    tau_sb = sb("tau_sb", [1, 2], F32)

    def chunk_src(i):
        if i < 32:
            return wup_h[i * 128:(i + 1) * 128, :]
        j = i - 32
        return wdn_h[(j // 4) * 128:(j // 4 + 1) * 128, (j % 4) * D:(j % 4 + 1) * D]

    scA = contextlib.ExitStack()
    sbA = lambda n, shp, dt=BF16: scA.enter_context(nc.sbuf_tensor(n, shp, dt))
    lnT = [sbA(f"lnT{d}", [128, S], BF16) for d in range(8)]
    wo_st = [sbA(f"wo_st{i}", [128, D], F32) for i in range(2)]
    wo_bf = sbA("wo_bf", [128, 8 * D], BF16)
    wkh_f = sbA("wkh_f", [128, 8 * 64], F32)
    wkh = sbA("wkh", [128, 8 * 64], BF16)
    wdq_f = sbA("wdq_f", [128, 8 * 32], F32)
    wdqb = sbA("wdqb", [128, 8 * 32], BF16)
    wuq_f = sbA("wuq_f", [32, D], F32)
    wuqb = sbA("wuqb", [32, D], BF16)
    wuk_f = sbA("wuk_f", [32, D], F32)
    wukb = sbA("wukb", [32, D], BF16)
    wuv_f = sbA("wuv_f", [32, D], F32)
    wuvb = sbA("wuvb", [32, D], BF16)
    wqr_f = sbA("wqr_f", [32, 256], F32)
    wqrb = sbA("wqrb", [32, 256], BF16)
    wuqT = sbA("wuqT", [128, 8 * 32], BF16)
    wukT = sbA("wukT", [128, 8 * 32], BF16)
    wuvT = sbA("wuvT", [128, 8 * 32], BF16)
    buq_f = sbA("buq_f", [128, 8], F32)
    buqb = sbA("buqb", [128, 8], BF16)
    buv_f = sbA("buv_f", [128, 8], F32)
    buvb = sbA("buvb", [128, 8], BF16)
    bo_sb = sbA("bo_sb", [1, D], F32)
    beff_f = sbA("beff_f", [1, D], F32)
    ln1b = [sbA(f"ln1b{i}", [128, D], BF16) for i in range(2)]
    wc_st = [sbA(f"wc_st{i}", [32, D], BF16) for i in range(2)]

    psA = contextlib.ExitStack()
    pkh = [psA.enter_context(nc.psum_tensor(f"pkh{i}", [64, 512], F32)) for i in range(2)]
    pcq = [psA.enter_context(nc.psum_tensor(f"pcq{i}", [32, 512], F32)) for i in range(2)]
    pmA = psA.enter_context(nc.psum_tensor("pmA", [128, 512], F32))
    pmB = psA.enter_context(nc.psum_tensor("pmB", [128, 512], F32))

    t_ones_row = b.v(lambda e: e.memset(ones_row[:], 1.0))
    t_ones_col = b.v(lambda e: e.memset(ones_col[:], 1.0))
    t_onesf_row = b.v(lambda e: e.memset(onesf_row[:], 1.0))
    t_accw = b.v(lambda e: e.memset(accw[:], 1.0))
    b.v(lambda e: e.memset(acc[:], 0.0))
    t_ckv1_i = b.v(lambda e: e.memset(ckv1[:], 0.0))
    _cap = ckv1[:, 32:33]
    t_ckv1_o = b.v(lambda e: e.memset(
        bass.AP(_cap.tensor, _cap.offset, [[128 * NK, 128], [128, NK]]), 1.0), t_ckv1_i)
    t_khT_z = b.v(lambda e: e.memset(khT[64:128, :], 0.0))
    t_lw_za = b.v(lambda e: e.memset(lwabs[32:64, :], 0.0))
    t_lw_z = b.v(lambda e: e.memset(lwabs[64:128, :], 0.0), t_lw_za)
    t_cq_za = b.v(lambda e: e.memset(cqT[32:64, :], 0.0))
    t_cq_z = b.v(lambda e: e.memset(cqT[64:128, :], 0.0), t_cq_za)

    d_wkh = []
    for dt in range(8):
        d_wkh.append(b.dma("sync", wkh_f[:, dt * 64:dt * 64 + 32],
                           wdkv_h[dt * 128:(dt + 1) * 128, :]))
        d_wkh.append(b.dma("sync", wkh_f[:, dt * 64 + 32:dt * 64 + 64],
                           wkr_h[dt * 128:(dt + 1) * 128, :]))
    t_wkh = b.v(lambda e: e.tensor_copy(wkh[:], wkh_f[:]), *d_wkh)
    d_wdq = [b.dma("sync", wdq_f[:, dt * 32:(dt + 1) * 32],
                   wdq_h[dt * 128:(dt + 1) * 128, :]) for dt in range(8)]
    t_wdq = b.v(lambda e: e.tensor_copy(wdqb[:], wdq_f[:]), *d_wdq)
    d_uq = b.dma("sync", wuq_f[:], wuq_h[:])
    t_uq = b.v(lambda e: e.tensor_copy(wuqb[:], wuq_f[:]), d_uq)
    d_uk = b.dma("sync", wuk_f[:], wuk_h[:])
    t_uk = b.v(lambda e: e.tensor_copy(wukb[:], wuk_f[:]), d_uk)
    d_uv = b.dma("sync", wuv_f[:], wuv_h[:])
    t_uv = b.v(lambda e: e.tensor_copy(wuvb[:], wuv_f[:]), d_uv)
    d_qr = b.dma("sync", wqr_f[:], wqr_h[:])
    t_qr = b.v(lambda e: e.tensor_copy(wqrb[:], wqr_f[:]), d_qr)
    d_bdq = b.dma("sync", bdq[:], bdq_h.rearrange("(p o) -> p o", o=1), slow=True)
    d_bkh1 = b.dma("sync", bkh[0:32, :], bdkv_h.rearrange("(p o) -> p o", o=1), slow=True)
    d_bkh2 = b.dma("sync", bkh[32:64, :], bkr_h.rearrange("(p o) -> p o", o=1), slow=True)
    d_buq = b.dma("sync", buq_f[:], buq_h.rearrange("(h p) -> p h", p=128), slow=True)
    t_buq = b.v(lambda e: e.tensor_copy(buqb[:], buq_f[:]), d_buq)
    d_buv = b.dma("sync", buv_f[:], buv_h.rearrange("(h p) -> p h", p=128), slow=True)
    t_buv = b.v(lambda e: e.tensor_copy(buvb[:], buv_f[:]), d_buv)
    d_bqr = b.dma("gpsimd", bias64[32:64, :], bqr_h.rearrange("(h p) -> p h", p=32), slow=True)
    d_bo = b.dma("sync", bo_sb[:], bo_h.rearrange("(o c) -> o c", o=1))
    t_wo = []
    for dt in range(8):
        d = b.dma("sync", wo_st[dt % 2][:], wo_h[dt * 128:(dt + 1) * 128, :],
                  t_wo[dt - 2] if dt >= 2 else None)
        t_wo.append(b.v(lambda e, dt=dt: e.tensor_copy(wo_bf[:, dt * D:(dt + 1) * D],
                                                       wo_st[dt % 2][:]), d))
    d_uqT, d_ukT, d_uvT = [], [], []
    for h in range(H):
        d_uqT.append(b.dma("sync", wuqT[:, h * 32:(h + 1) * 32],
                           wuqb[:, h * 128:(h + 1) * 128], t_uq, transpose=True))
        d_ukT.append(b.dma("sync", wukT[:, h * 32:(h + 1) * 32],
                           wukb[:, h * 128:(h + 1) * 128], t_uk, transpose=True))
        d_uvT.append(b.dma("sync", wuvT[:, h * 32:(h + 1) * 32],
                           wuvb[:, h * 128:(h + 1) * 128], t_uv, transpose=True))

    d_x = [None] * NT
    s_ln1 = [None] * NT
    g_ln1 = [None] * NT
    wload_tok = [None] * 64
    wred_tok = [None] * 64
    win1_user = [None, None]
    wredu = [None]

    def p1_load(ci):
        wload_tok[ci] = b.dma("sync", w_in[ci % 2][:], chunk_src(ci), win1_user[ci % 2])

    def p1_red(ci):
        col = 0 if ci < 32 else 1
        r1 = b.v(lambda e: e.reduce_sum(wred[:], w_in[ci % 2][:], axis=AX.X,
                                        apply_absolute_value=True),
                 wload_tok[ci], wredu[0])
        r2 = b.v(lambda e, col=col: e.tensor_tensor(acc[:, col:col + 1],
                                                    acc[:, col:col + 1], wred[:],
                                                    ALU.add), r1)
        wred_tok[ci] = r2
        win1_user[ci % 2] = r2
        wredu[0] = r2

    wl_ct = [0]
    wr_ct = [0]

    def p1_pump(target):
        while wl_ct[0] < min(64, target):
            ci = wl_ct[0]
            if ci >= 2:
                p1_red(ci - 2)
                wr_ct[0] = ci - 1
            p1_load(ci)
            wl_ct[0] += 1

    for i in range(NT):
        d_x[i] = b.dma("sync", x_sb[i % 2][:], x_h[i * 128:(i + 1) * 128, :],
                       s_ln1[i - 2] if i >= 2 else None)
        p, st = pool[i % 2], stt[i % 2]
        v1 = b.v(lambda e, st=st, i=i: e.bn_stats(st[:, 0:6], x_sb[i % 2][:, 0:512]),
                 d_x[i], s_ln1[i - 2] if i >= 2 else None)
        v2 = b.v(lambda e, st=st, i=i: e.bn_stats(st[:, 6:12], x_sb[i % 2][:, 512:1024]), d_x[i])
        v3 = b.v(lambda e, st=st, p=p: e.bn_aggr(p[:, 0:2], st[:]), v1, v2)
        v4 = b.v(lambda e, p=p: e.tensor_scalar(p[:, 2:3], p[:, 1:2], LN_EPS, None, ALU.add), v3)
        s1 = b.s(lambda e, p=p: e.activation(p[:, 3:4], p[:, 2:3], AF.Ln), v4)
        s2 = b.s(lambda e, p=p: e.activation(p[:, 4:5], p[:, 3:4], AF.Exp, scale=-0.5), s1)
        v5 = b.v(lambda e, p=p: e.tensor_scalar(p[:, 5:6], p[:, 0:1], p[:, 4:5], -1.0,
                                                ALU.mult, ALU.mult), s2)
        s_ln1[i] = b.s(lambda e, p=p, i=i: e.activation(ln1b[i % 2][:], x_sb[i % 2][:],
                                                        AF.Identity, scale=p[:, 4:5],
                                                        bias=p[:, 5:6]),
                       v5, g_ln1[i - 2] if i >= 2 else None)
        g_ln1[i] = b.dma("gpsimd", ln1s[i * 128:(i + 1) * 128, :], ln1b[i % 2][:], s_ln1[i])
        p1_pump(-(-64 * (i + 1)) // NT)
    p1_pump(64)
    while wr_ct[0] < 64:
        p1_red(wr_ct[0])
        wr_ct[0] += 1
    wred_all = wred_tok
    gf = b.fence("gpsimd")
    d_lnT = [b.dma("sync", lnT[dt][:], ln1s[0:S, dt * 128:(dt + 1) * 128],
                   *gf, transpose=True) for dt in range(8)]

    t_khT_ev = [None] * NQ
    t_cqT_ev = [None] * NQ
    prev_kh = [None, None]
    prev_cq = [None, None]
    for blk in range(NQ):
        sl = slice(blk * 512, (blk + 1) * 512)
        mm = None
        for dt in range(8):
            mm = b.t(lambda e, dt=dt, blk=blk, sl=sl: e.matmul(
                pkh[blk % 2][:], wkh[:, dt * 64:(dt + 1) * 64], lnT[dt][:, sl],
                start=(dt == 0), stop=(dt == 7), skip_group_check=True),
                d_lnT[dt], t_wkh, prev_kh[blk % 2] if dt == 0 else None)
        t_khT_ev[blk] = b.v(lambda e, blk=blk, sl=sl: e.tensor_scalar(
            khT[0:64, sl], pkh[blk % 2][:], bkh[:], None, ALU.add), mm, d_bkh1, d_bkh2)
        prev_kh[blk % 2] = t_khT_ev[blk]
        mm = None
        for dt in range(8):
            mm = b.t(lambda e, dt=dt, blk=blk, sl=sl: e.matmul(
                pcq[blk % 2][:], wdqb[:, dt * 32:(dt + 1) * 32], lnT[dt][:, sl],
                start=(dt == 0), stop=(dt == 7), skip_group_check=True),
                d_lnT[dt], t_wdq, prev_cq[blk % 2] if dt == 0 else None)
        t_cqT_ev[blk] = b.v(lambda e, blk=blk, sl=sl: e.tensor_scalar(
            cqT[0:32, sl], pcq[blk % 2][:], bdq[:], None, ALU.add), mm, d_bdq, t_cq_z)
        prev_cq[blk % 2] = t_cqT_ev[blk]
    d_ckv = [b.dma("sync", ckv1[:, kt * 128:kt * 128 + 32],
                   khT[0:32, kt * 128:(kt + 1) * 128],
                   t_khT_ev[(kt * 128) // 512], t_ckv1_o, transpose=True)
             for kt in range(NK)]
    ckv_w = d_ckv[-8:]

    g_wc = []
    last_pmA = None
    last_pmB = None
    t_lwabs = []
    for h in range(H):
        mm = b.t(lambda e, h=h: e.matmul(pmA[0:32, 0:32], wuqT[:, h * 32:(h + 1) * 32],
                                         wukT[:, h * 32:(h + 1) * 32], start=True, stop=True),
                 d_uqT[h], d_ukT[h], last_pmA)
        last_pmA = b.v(lambda e, h=h: e.tensor_copy(lwabs[0:32, h * 64:h * 64 + 32],
                                                    pmA[0:32, 0:32]), mm, t_lw_z)
        t_lwabs.append(last_pmA)
        t_lwabs.append(b.v(lambda e, h=h: e.tensor_copy(
            lwabs[0:32, h * 64 + 32:h * 64 + 64], wqrb[:, h * 32:(h + 1) * 32]), t_qr))
        mm = b.t(lambda e, h=h: e.matmul(pmB[0:32, 0:1], wukT[:, h * 32:(h + 1) * 32],
                                         buqb[:, h:h + 1], start=True, stop=True),
                 d_ukT[h], t_buq, last_pmB)
        last_pmB = b.v(lambda e, h=h: e.tensor_copy(bias64[0:32, h:h + 1], pmB[0:32, 0:1]), mm)
        for ch in range(2):
            mm = b.t(lambda e, h=h, ch=ch: e.matmul(
                pmA[0:32, :], wuvT[:, h * 32:(h + 1) * 32],
                wo_bf[:, h * D + ch * 512:h * D + (ch + 1) * 512], start=True, stop=True),
                d_uvT[h], t_wo[h], last_pmA)
            last_pmA = b.v(lambda e, h=h, ch=ch: e.tensor_copy(
                wc_st[h % 2][:, ch * 512:(ch + 1) * 512], pmA[0:32, :]), mm,
                g_wc[h - 2] if h >= 2 else None)
        g_wc.append(b.dma("gpsimd", wc_pack[h // 4][(h % 4) * 32:(h % 4 + 1) * 32, :],
                          wc_st[h % 2][:], last_pmA))
    t_beff = []
    for ch in range(2):
        mm = None
        for kt in range(8):
            mm = b.t(lambda e, kt=kt, ch=ch: e.matmul(
                pmB[0:1, :], buvb[:, kt:kt + 1],
                wo_bf[:, kt * D + ch * 512:kt * D + (ch + 1) * 512],
                start=(kt == 0), stop=(kt == 7), skip_group_check=True),
                t_buv, t_wo[-1], last_pmB if kt == 0 else None)
        last_pmB = b.v(lambda e, ch=ch: e.tensor_tensor(
            beff_f[0:1, ch * 512:(ch + 1) * 512], pmB[0:1, :],
            bo_sb[0:1, ch * 512:(ch + 1) * 512], ALU.add), mm, d_bo)
        t_beff.append(last_pmB)
    t_beff_bf = b.v(lambda e: e.tensor_copy(beff[:], beff_f[:]), t_beff[0], t_beff[1])


    mm1 = b.t(lambda e: e.matmul(pmA[0:1, 0:2], accw[:], acc[:],
                                 start=True, stop=True, skip_group_check=True),
              wred_all[-1], *wred_all, t_accw, last_pmA)
    vv = b.v(lambda e: e.tensor_copy(tau_sb[:], pmA[0:1, 0:2]), mm1)
    mm2 = b.t(lambda e: e.matmul(pmA[0:128, 0:2], onesf_row[:], tau_sb[:],
                                 start=True, stop=True, skip_group_check=True),
              vv, t_onesf_row)
    v2f = b.v(lambda e: e.tensor_scalar(tau_p[:], pmA[0:128, 0:2],
                                        0.5 / float(DFF * D), None, ALU.mult), mm2)
    fin_tok = b.v(lambda e: e.tensor_scalar(tau_n[:], tau_p[:], -1.0, None, ALU.mult), v2f)

    b.barrier()
    scA.close()
    psA.close()
    scB = contextlib.ExitStack()
    sbB = lambda n, shp, dt=BF16: scB.enter_context(nc.sbuf_tensor(n, shp, dt))
    wuT = [sbB(f"wuT{d}", [128, DFF], BF16) for d in range(8)]
    qhT = [sbB(f"qhT{h}", [128, 512], BF16) for h in range(H)]
    t_qhT_z = [b.v(lambda e, h=h: e.memset(qhT[h][64:128, :], 0.0)) for h in range(H)]
    pT = [sbB(f"pT{i}", [128, 1024], BF16) for i in range(2)]
    ovtn = [[sbB(f"ovtn{q}_{j}", [128, 512], BF16) for j in range(2)] for q in range(2)]
    ovst = [sbB(f"ovst{i}", [32, 512], BF16) for i in range(2)]
    rl_bf = sbB("rl_bf", [1, 512], BF16)
    rl_f = sbB("rl_f", [1, 512], F32)
    rb_sb = sbB("rb_sb", [32, 512], BF16)
    attn_sb = [sbB("attn0", [128, D], F32)] * 2
    xn_bf = [sbB("xnbf0", [128, D], BF16)] * 2
    hT_sb = [sbB(f"hT{i}", [128, 32 * 256], BF16) for i in range(2)]
    NRING = 2
    wdR = [sbB(f"wdR{r}", [128, D], BF16) for r in range(NRING)]
    g2 = [sbB(f"g2_{i}", [128, 256], BF16) for i in range(2)]
    xnT_in = [sbB(f"xnT{i}", [128, 8 * 256], BF16) for i in range(2)]
    x2_in = [sbB(f"x2in{i}", [128, D], F32) for i in range(2)]
    outt = [sbB(f"outt{i}", [128, D], F32) for i in range(2)]
    ssq = sbB("ssq", [128, 2], F32)
    lssq = sbB("lssq", [128, 2], F32)
    sqs = sbB("sqs", [128, 2], F32)
    dens = sbB("dens", [128, 2], F32)
    scaleT = [sbB(f"scaleT{i}", [128, 2], F32) for i in range(2)]
    pool2 = [sbB(f"pool2_{i}", [128, 16], F32) for i in range(2)]
    stt2 = [sbB(f"stt2_{i}", [128, 12], F32) for i in range(2)]

    psB = contextlib.ExitStack()
    arena = [psB.enter_context(nc.psum_tensor(f"arena{i}", [128, 1024], F32)) for i in range(2)]
    povT = psB.enter_context(nc.psum_tensor("povT", [128, 512], F32))
    spare = psB.enter_context(nc.psum_tensor("spare", [128, 512], F32))
    ph = [psB.enter_context(nc.psum_tensor(f"ph{i}", [128, 512], F32)) for i in range(2)]

    bank = {"spare": None, "povT": None}
    arena_free = [None, None]
    busers = {"ph0": None, "ph1": None, "spare": None, "povT": None}
    bmap = {"ph0": ph[0], "ph1": ph[1], "spare": spare, "povT": povT}
    util_ptr = [0]
    up_ptr = [0]
    dummy_tok = [None]

    def util_next():
        k = ("spare", "ph1")[util_ptr[0]]
        util_ptr[0] ^= 1
        return k

    def up_next():
        k = ("ph0", "ph1", "povT")[up_ptr[0]]
        up_ptr[0] = (up_ptr[0] + 1) % 3
        return k

    NG2 = NK // 2

    ws = {"load": [None] * 64, "tern": [None] * 64, "write": [None] * 64,
          "win": [wred_tok[62], wred_tok[63]], "wtu": [None] * 2, "tmpu": None}

    def chunk_src(i):
        if i < 32:
            return wup_h[i * 128:(i + 1) * 128, :]
        j = i - 32
        return wdn_h[(j // 4) * 128:(j // 4 + 1) * 128, (j % 4) * D:(j % 4 + 1) * D]

    def chunk_dst(i):
        if i < 32:
            return wups[i * 128:(i + 1) * 128, :]
        j = i - 32
        return wdns[(j // 4) * 128:(j // 4 + 1) * 128, (j % 4) * D:(j % 4 + 1) * D]

    def mk_p2_load(i):
        def f():
            ws["load"][i] = b.dma("sync", w_in[i % 2][:], chunk_src(i), ws["win"][i % 2])
        return f

    def mk_p2_tern(i):
        def f():
            col = 0 if i < 32 else 1
            t1 = b.v(lambda e, col=col: e.tensor_scalar(
                wt_tmp[:], w_in[i % 2][:], tau_n[:, col:col + 1], None, ALU.is_lt),
                ws["load"][i], fin_tok, ws["tmpu"], ws["wtu"][i % 2])
            t2 = b.v(lambda e, col=col: e.scalar_tensor_tensor(
                wt_bf[i % 2][:], w_in[i % 2][:], tau_p[:, col:col + 1], wt_tmp[:],
                ALU.is_gt, ALU.subtract), t1)
            ws["win"][i % 2] = t2
            ws["tmpu"] = t2
            g1 = b.dma("gpsimd", chunk_dst(i), wt_bf[i % 2][:], t2)
            ws["write"][i] = g1
            ws["wtu"][i % 2] = g1
        return f

    d_wuT = [None] * 8

    def mk_read_up(dt):
        def f():
            d_wuT[dt] = b.dma("sync", wuT[dt][:], wups[0:DFF, dt * 128:(dt + 1) * 128],
                              *ws["write"][0:32], transpose=True)
        return f

    wtasks = []
    for i in range(64):
        if i >= 2:
            wtasks.append(mk_p2_tern(i - 2))
        wtasks.append(mk_p2_load(i))
    for i in range(62, 64):
        wtasks.append(mk_p2_tern(i))
    for dt in range(8):
        wtasks.append(mk_read_up(dt))
    wti = [0]

    def pump_wtasks(n):
        k = 0
        while wti[0] < len(wtasks) and k < n:
            wtasks[wti[0]]()
            wti[0] += 1
            k += 1

    d_xnT_toks = [None, None]
    xnT_user = [None, None]
    hT_user = [None, None]
    hev = {}
    gelu_tok = {}
    g2_user = [None] * 2
    sq_last = {}
    ss_last = {}
    scaleT_tok = {}
    scaleT_user = [None, None]
    ring_user = [None] * NRING
    ring_tok = {}
    x2i_user = [None, None]
    outt_user = [None, None]
    p2pool_user = [None, None]
    p2stt_user = [None, None]
    g_out = []

    def mk_xnt_reads(tb):
        def f():
            par = (tb // 1) % 2
            toks = []
            for dt in range(8):
                toks.append(b.dma(
                    "sync",
                    xnT_in[par][:, dt * 256:(dt + 1) * 256],
                    xns[tb * 256:(tb + 1) * 256, dt * 128:(dt + 1) * 128],
                    g_xn[tb * 2], g_xn[tb * 2 + 1],
                    xnT_user[par] if dt == 0 else None, transpose=True))
            d_xnT_toks[par] = toks
        return f

    def mk_up(tb, ft):
        def f():
            par = tb % 2
            pk = up_next()
            pb = bmap[pk]
            mm = None
            for dt in range(8):
                mm = b.t(lambda e, dt=dt, ft=ft, par=par, pb=pb: e.matmul(
                    pb[:, 0:256], wuT[dt][:, ft * 128:(ft + 1) * 128],
                    xnT_in[par][:, dt * 256:(dt + 1) * 256],
                    start=(dt == 0), stop=(dt == 7), skip_group_check=True),
                    d_xnT_toks[par][dt] if ft == 0 else None,
                    d_wuT[dt],
                    busers[pk] if dt == 0 else None)
            ge = b.s(lambda e, ft=ft, par=par, pb=pb: e.activation(
                hT_sb[par][:, ft * 256:(ft + 1) * 256], pb[:, 0:256], AF.Gelu),
                mm, hT_user[par] if ft == 0 else None)
            busers[pk] = ge
            hev[(tb, ft)] = ge
            if ft == 31:
                xnT_user[par] = mm
        return f

    def mk_sqss(tb, ft):
        def f():
            par = tb % 2
            sq = b.v(lambda e, ft=ft, par=par: e.tensor_mul(
                g2[ft % 2][:], hT_sb[par][:, ft * 256:(ft + 1) * 256],
                hT_sb[par][:, ft * 256:(ft + 1) * 256]),
                hev[(tb, ft)], g2_user[ft % 2])
            tok = None
            for ts2 in range(2):
                tok = b.t(lambda e, ft=ft, ts2=ts2: e.matmul(
                    spare[:, ts2:ts2 + 1], g2[ft % 2][:, ts2 * 128:(ts2 + 1) * 128],
                    ones_col[:], start=(ft == 0 and ts2 == 0),
                    stop=(ft == 31 and ts2 == 1), skip_group_check=True),
                    sq, t_ones_col,
                    busers["spare"] if (ft == 0 and ts2 == 0) else None)
            g2_user[ft % 2] = tok
            ss_last[tb] = tok
        return f

    def mk_rms(tb, idx):
        def f():
            c1 = b.v(lambda e: e.tensor_copy(ssq[:], spare[:, 0:2]), ss_last[tb],
                     scaleT_user[idx])
            bank["spare"] = c1
            busers["spare"] = c1
            c2 = b.s(lambda e: e.activation(lssq[:], ssq[:], AF.Ln), c1)
            c3 = b.s(lambda e: e.activation(sqs[:], lssq[:], AF.Exp, scale=0.5), c2)
            c4 = b.v(lambda e: e.tensor_scalar(dens[:], sqs[:], 1.0 / 64.0, RMS_EPS,
                                               ALU.mult, ALU.add), c3)
            c5 = b.v(lambda e, idx=idx: e.reciprocal(scaleT[idx][:], dens[:]), c4)
            scaleT_tok[tb] = c5
        return f

    dp_state = {}

    def mk_down_mms(ta, tb_):
        def f():
            accs = [ph[0][:], ph[1][:], povT[:], spare[:],
                    arena[0][:, 0:512], arena[0][:, 512:1024],
                    arena[1][:, 0:512], arena[1][:, 512:1024]]
            acc_w = [busers["ph0"], busers["ph1"], busers["povT"], busers["spare"],
                     arena_free[0], arena_free[0], arena_free[1], arena_free[1]]
            ring_tok = [None] * 32

            def ring_read(ft):
                r = ft % NRING
                ring_tok[ft] = b.dma("sync", wdR[r][:],
                                     wdns[0:D, ft * 128:(ft + 1) * 128],
                                     *ws["write"][32:64], ring_user[r], transpose=True)

            for ft in range(NRING):
                ring_read(ft)
            mmj = [None] * 8
            for ft in range(32):
                for j in range(8):
                    tb = (ta, tb_)[j // 4]
                    par = tb % 2
                    tsub = (j // 2) % 2
                    ch = j % 2
                    mmj[j] = b.t(lambda e, ft=ft, j=j, par=par, tsub=tsub, ch=ch: e.matmul(
                        accs[j],
                        hT_sb[par][:, ft * 256 + tsub * 128:ft * 256 + (tsub + 1) * 128],
                        wdR[ft % NRING][:, ch * 512:(ch + 1) * 512],
                        start=(ft == 0), stop=(ft == 31), skip_group_check=True),
                        hev[(tb, ft)] if j % 4 == 0 else None,
                        ring_tok[ft] if j == 0 else None,
                        acc_w[j] if ft == 0 else None)
                ring_user[ft % NRING] = mmj[7]
                if ft + NRING < 32:
                    ring_read(ft + NRING)
            hT_user[ta % 2] = mmj[3]
            hT_user[tb_ % 2] = mmj[7]
            dp_state["mmj7"] = mmj[7]
            dp_state["accs"] = accs
            dp_state["evs"] = [None] * 8
            dp_state["d_x2"] = [None] * 4
            dp_state["v4"] = [None] * 4
        return f

    def mk_tail_evac(ta, tb_, q):
        accs, evs, d_x2 = dp_state["accs"], dp_state["evs"], dp_state["d_x2"]
        tb = (ta, tb_)[q // 2]
        tsub = q % 2
        idx = q // 2
        d_x2[q] = b.dma("gpsimd", x2_in[q % 2][:],
                        x2s[tb * 256 + tsub * 128:tb * 256 + (tsub + 1) * 128, :],
                        g_x2[tb * 2 + tsub], x2i_user[q % 2])
        for ch in range(2):
            j = q * 2 + ch
            evs[j] = b.v(lambda e, q=q, ch=ch, j=j, idx=idx, tsub=tsub, accs=accs: e.tensor_scalar(
                outt[q % 2][:, ch * 512:(ch + 1) * 512], accs[j],
                scaleT[idx][:, tsub:tsub + 1], None, ALU.mult),
                dp_state["mmj7"], scaleT_tok[tb], outt_user[q % 2] if ch == 0 else None)
        scaleT_user[idx] = evs[q * 2 + 1]

    def mk_tail_stats(q):
        evs = dp_state["evs"]
        p, st = pool2[q % 2], stt2[q % 2]
        v1 = b.v(lambda e, st=st, q=q: e.bn_stats(
            st[:, 0:6], outt[q % 2][:, 0:512]), evs[q * 2], p2stt_user[q % 2])
        v2 = b.v(lambda e, st=st, q=q: e.bn_stats(
            st[:, 6:12], outt[q % 2][:, 512:1024]), evs[q * 2 + 1])
        v3 = b.v(lambda e, st=st, p=p: e.bn_aggr(p[:, 0:2], st[:]), v1, v2,
                 p2pool_user[q % 2])
        p2stt_user[q % 2] = v3
        v4 = b.v(lambda e, p=p: e.tensor_scalar(p[:, 2:3], p[:, 1:2], LN_EPS,
                                                None, ALU.add), v3)
        dp_state["v4"][q] = v4

    def mk_tail_apply(ta, tb_, q):
        d_x2 = dp_state["d_x2"]
        tb = (ta, tb_)[q // 2]
        tsub = q % 2
        gt = tb * 2 + tsub
        p = pool2[q % 2]
        s1 = b.s(lambda e, p=p: e.activation(p[:, 3:4], p[:, 2:3], AF.Ln),
                 dp_state["v4"][q])
        s2 = b.s(lambda e, p=p: e.activation(p[:, 4:5], p[:, 3:4], AF.Exp,
                                             scale=-0.5), s1)
        v5 = b.v(lambda e, p=p: e.tensor_scalar(p[:, 5:6], p[:, 0:1], p[:, 4:5],
                                                -1.0, ALU.mult, ALU.mult), s2)
        v6 = b.v(lambda e, q=q, p=p: e.tensor_scalar(
            outt[q % 2][:], outt[q % 2][:], p[:, 0:1], p[:, 4:5],
            ALU.subtract, ALU.mult), v5)
        p2pool_user[q % 2] = v6
        v7 = b.v(lambda e, q=q: e.tensor_tensor(
            outt[q % 2][:], outt[q % 2][:], x2_in[q % 2][:], ALU.add),
            v6, d_x2[q])
        x2i_user[q % 2] = v7
        gdo = b.dma("gpsimd", out_h[gt * 128:(gt + 1) * 128, :], outt[q % 2][:], v7)
        outt_user[q % 2] = gdo
        g_out.append(gdo)

    def mk_tails(ta, tb_):
        def f_evac01():
            mk_tail_evac(ta, tb_, 0)
            mk_tail_evac(ta, tb_, 1)
            evs = dp_state["evs"]
            busers["ph0"] = evs[0]
            busers["ph1"] = evs[1]
            busers["povT"] = evs[2]
            bank["povT"] = evs[2]
            busers["spare"] = evs[3]
            bank["spare"] = evs[3]

        def f_chain01():
            mk_tail_stats(0)
            mk_tail_stats(1)
            mk_tail_apply(ta, tb_, 0)
            mk_tail_apply(ta, tb_, 1)

        def f_q23():
            evs = dp_state["evs"]
            mk_tail_evac(ta, tb_, 2)
            mk_tail_evac(ta, tb_, 3)
            mk_tail_stats(2)
            mk_tail_stats(3)
            mk_tail_apply(ta, tb_, 2)
            mk_tail_apply(ta, tb_, 3)
            arena_free[0] = evs[5]
            arena_free[1] = evs[7]
        return (f_evac01, f_chain01, f_q23)

    def ffn_work(tbs):
        work = []
        ta, tb_ = tbs
        work.append(("fine", mk_xnt_reads(ta)))
        work.append(("slot0", None))
        for ft in range(32):
            work.append(("fine", mk_up(ta, ft)))
            if ft == 8:
                work.append(("slot1", None))
            if ft == 16:
                work.append(("fine", mk_xnt_reads(tb_)))
            if ft == 21:
                work.append(("slot2", None))
        for ft in range(32):
            work.append(("fine", mk_sqss(ta, ft)))
        work.append(("fine", mk_rms(ta, 0)))
        for ft in range(32):
            work.append(("fine", mk_up(tb_, ft)))
            if ft >= 2:
                work.append(("fine", mk_sqss(tb_, ft - 2)))
        work.append(("fine", mk_sqss(tb_, 30)))
        work.append(("fine", mk_sqss(tb_, 31)))
        work.append(("fine", mk_rms(tb_, 1)))
        work.append(("macro", mk_down_mms(ta, tb_)))
        return work

    arena_exp = arena_free
    pT_user = [None, None]
    ovst_user = [None, None]
    ovtn_user = [[None, None], [None, None]]
    attn_user = AliasTracker()
    xn_user = AliasTracker()
    xload = [None] * NT
    g_x2 = [None] * NT
    g_xn = [None] * NT
    xsb_user = [[s_ln1[NT - 2]], [s_ln1[NT - 1]]]
    pool_user = [s_ln1[NT - 2], s_ln1[NT - 1]]

    work = []
    wk = [0]

    def fines_left():
        n = 0
        for i in range(wk[0], len(work)):
            if work[i][0] == "macro":
                n += 0
            else:
                n += 1
        return n

    def pump_fine(n):
        k = 0
        while wk[0] < len(work) and k < n:
            kind, fn = work[wk[0]]
            if kind == "macro":
                break
            fn()
            wk[0] += 1
            k += 1

    def pump_boundary():
        if wk[0] < len(work) and work[wk[0]][0] == "macro":
            work[wk[0]][1]()
            wk[0] += 1

    def pump_all():
        while wk[0] < len(work):
            work[wk[0]][1]()
            wk[0] += 1

    def mk_dummy():
        tok = b.t(lambda e: e.matmul(
            ph[0][:], ckv1[:, 0:128], khT[:, 0:512],
            start=True, stop=True, skip_group_check=True),
            t_khT_ev[0], d_ckv[0], t_khT_z, dummy_tok[0], busers["ph0"])
        dummy_tok[0] = tok
        busers["ph0"] = tok

    def mk_qb_tail(qb, ovn_gds):
        items = []
        qpar = qb % 2

        def mk_itemA(tt):
            def f():
                gt = qb * 4 + tt
                tsl = slice(tt * 128, (tt + 1) * 128)
                p, st = pool[tt % 2], stt[tt % 2]
                pw = pool_user[tt % 2]
                pks = [None, None]
                mms = [None, None]
                for ch in range(2):
                    csl = slice(ch * 512, (ch + 1) * 512)
                    pk = util_next()
                    pb = bmap[pk]
                    pks[ch] = pk
                    b.t(lambda e, tsl=tsl, csl=csl, pb=pb: e.matmul(
                        pb[:], ovtn[qpar][0][:, tsl], wc_pack[0][:, csl],
                        start=True, stop=False, skip_group_check=True),
                        *ovn_gds, *g_wc, busers[pk])
                    b.t(lambda e, tsl=tsl, csl=csl, pb=pb: e.matmul(
                        pb[:], ovtn[qpar][1][:, tsl], wc_pack[1][:, csl],
                        start=False, stop=False, skip_group_check=True))
                    mms[ch] = b.t(lambda e, csl=csl, pb=pb: e.matmul(
                        pb[:], ones_row[0:1, :], beff[0:1, csl],
                        start=False, stop=True, skip_group_check=True), t_beff_bf)
                    ovtn_user[qpar][0] = mms[ch]
                    ovtn_user[qpar][1] = mms[ch]
                v1 = b.v(lambda e, st=st: e.bn_stats(st[:, 0:6], bmap[pks[0]][:]),
                         mms[0], pw)
                v2 = b.v(lambda e, st=st: e.bn_stats(st[:, 6:12], bmap[pks[1]][:]), mms[1])
                v3 = b.v(lambda e, st=st, p=p: e.bn_aggr(p[:, 0:2], st[:]), v1, v2)
                v4 = b.v(lambda e, p=p: e.tensor_scalar(p[:, 2:3], p[:, 1:2], LN_EPS, None,
                                                        ALU.add), v3)
                s1 = b.s(lambda e, p=p: e.activation(p[:, 3:4], p[:, 2:3], AF.Ln), v4)
                s2 = b.s(lambda e, p=p: e.activation(p[:, 4:5], p[:, 3:4], AF.Exp,
                                                     scale=-0.5), s1)
                v5 = b.v(lambda e, p=p: e.tensor_scalar(p[:, 5:6], p[:, 0:1], p[:, 4:5],
                                                        -1.0, ALU.mult, ALU.mult), s2)
                ap0 = b.v(lambda e, tt=tt, p=p: e.tensor_scalar(
                    attn_sb[tt % 2][:, 0:512], bmap[pks[0]][:], p[:, 0:1], p[:, 4:5],
                    ALU.subtract, ALU.mult), v5, attn_user[tt % 2])
                ap1 = b.v(lambda e, tt=tt, p=p: e.tensor_scalar(
                    attn_sb[tt % 2][:, 512:1024], bmap[pks[1]][:], p[:, 0:1], p[:, 4:5],
                    ALU.subtract, ALU.mult), v5)
                busers[pks[0]] = ap0
                busers[pks[1]] = ap1
                v7 = b.v(lambda e, tt=tt, gt=gt: e.tensor_tensor(
                    x_sb[gt % 2][:], attn_sb[tt % 2][:], x_sb[gt % 2][:], ALU.add),
                    ap0, ap1, xload[gt])
                attn_user[tt % 2] = v7
                g_x2[gt] = b.dma("gpsimd", x2s[gt * 128:(gt + 1) * 128, :],
                                 x_sb[gt % 2][:], v7)
                tailst["v7"][tt] = v7
                tailst["v3"][tt] = v3
            return f

        def mk_itemB(tt):
            def f():
                gt = qb * 4 + tt
                p, st = pool[tt % 2], stt[tt % 2]
                v7 = tailst["v7"][tt]
                u1 = b.v(lambda e, st=st, gt=gt: e.bn_stats(st[:, 0:6],
                                                            x_sb[gt % 2][:, 0:512]),
                         v7, tailst["v3"][tt])
                u2 = b.v(lambda e, st=st, gt=gt: e.bn_stats(st[:, 6:12],
                                                            x_sb[gt % 2][:, 512:1024]), v7)
                u3 = b.v(lambda e, st=st, p=p: e.bn_aggr(p[:, 6:8], st[:]), u1, u2)
                u4 = b.v(lambda e, p=p: e.tensor_scalar(p[:, 8:9], p[:, 7:8], LN_EPS, None,
                                                        ALU.add), u3)
                w1 = b.s(lambda e, p=p: e.activation(p[:, 9:11], p[:, 7:9], AF.Ln), u4)
                w2 = b.s(lambda e, p=p: e.activation(p[:, 11:13], p[:, 9:11], AF.Exp,
                                                     scale=0.5), w1)
                u5 = b.v(lambda e, p=p: e.scalar_tensor_tensor(p[:, 13:14], p[:, 12:13],
                                                               RMS_EPS, p[:, 11:12],
                                                               ALU.mult, ALU.add), w2)
                u6 = b.v(lambda e, p=p: e.reciprocal(p[:, 14:15], p[:, 13:14]), u5)
                u7 = b.v(lambda e, p=p: e.tensor_scalar(p[:, 15:16], p[:, 6:7], p[:, 14:15],
                                                        -1.0, ALU.mult, ALU.mult), u6)
                sxn = b.v(lambda e, tt=tt, gt=gt, p=p: e.tensor_scalar(
                    xn_bf[tt % 2][:], x_sb[gt % 2][:], p[:, 6:7], p[:, 14:15],
                    ALU.subtract, ALU.mult), u7, xn_user[tt % 2])
                pool_user[tt % 2] = u7
                g_xn[gt] = b.dma("gpsimd", xns[gt * 128:(gt + 1) * 128, :],
                                 xn_bf[tt % 2][:], sxn)
                xn_user[tt % 2] = g_xn[gt]
                xsb_user[gt % 2] = [sxn, g_x2[gt]]
                g2l = gt + 2
                if g2l < NT:
                    xload[g2l] = b.dma("sync", x_sb[g2l % 2][:],
                                       x_h[g2l * 128:(g2l + 1) * 128, :],
                                       *xsb_user[g2l % 2])
            return f

        for tt in range(4):
            items.append(("fine", mk_itemA(tt)))
            items.append(("fine", mk_itemB(tt)))
        return items


    def weave(tail_items, ffn_items):
        if not tail_items:
            return ffn_items
        cut = 34
        return (tail_items[0:4] + ffn_items[0:1]
                + tail_items[4:8] + ffn_items[1:cut]
                + ffn_items[cut:])

    tailst = {"v7": [None] * 4, "v3": [None] * 4}
    pending_tail = []

    for qb in range(NQ):
        assert wk[0] == len(work), f"ffn backlog {len(work)-wk[0]} at qb{qb}"
        work = pending_tail
        pending_tail = []
        wk[0] = 0
        qsl = slice(qb * 512, (qb + 1) * 512)
        if qb == 0:
            for tt in range(2):
                xload[tt] = b.dma("sync", x_sb[tt % 2][:], x_h[tt * 128:(tt + 1) * 128, :],
                                  *xsb_user[tt % 2])
        qh_ev = [None] * H
        for h in range(H):
            pk = util_next()
            pb = bmap[pk]
            mmq = b.t(lambda e, h=h, pb=pb, qsl=qsl: e.matmul(
                pb[0:64, :], lwabs[:, h * 64:(h + 1) * 64], cqT[:, qsl],
                start=True, stop=True, skip_group_check=True),
                t_cqT_ev[qb], busers[pk], t_lwabs[-1])
            qh_ev[h] = b.v(lambda e, h=h, pb=pb: e.tensor_scalar(
                qhT[h][0:64, :], pb[0:64, :], bias64[:, h:h + 1], None, ALU.add),
                mmq, last_pmB, d_bqr, t_qhT_z[h])
            busers[pk] = qh_ev[h]

        pend_tail = None
        ovn_gds = []
        for h in range(H):
            sc_tok = [None] * NG2
            e_tok = [None] * NG2

            def emit_S(g, h=h, qb=qb):
                tok = None
                for j in range(2):
                    kt = 2 * g + j
                    tok = b.t(lambda e, kt=kt, j=j, g=g, h=h: e.matmul(
                        arena[g % 2][:, j * 512:(j + 1) * 512],
                        khT[:, kt * 128:(kt + 1) * 128], qhT[h][:],
                        start=True, stop=True, skip_group_check=True),
                        qh_ev[h] if j == 0 else None,
                        t_khT_ev[-1] if (h == 0 and qb == 0 and j == 0) else None,
                        arena_free[g % 2] if j == 0 else None)
                sc_tok[g] = tok

            def emit_E(g, h=h):
                e_tok[g] = b.s(lambda e, g=g: e.activation(
                    pT[g % 2][:], arena[g % 2][:], AF.Exp, scale=SCALE),
                    sc_tok[g], pT_user[g % 2])
                arena_free[g % 2] = e_tok[g]

            def emit_O(g, h=h):
                tok = None
                for j in range(2):
                    kt = 2 * g + j
                    tok = b.t(lambda e, kt=kt, j=j, g=g: e.matmul(
                        povT[:], ckv1[:, kt * 128:(kt + 1) * 128],
                        pT[g % 2][:, j * 512:(j + 1) * 512],
                        start=(kt == 0), stop=(kt == NK - 1), skip_group_check=True),
                        e_tok[g] if j == 0 else None,
                        *(ckv_w if kt == 0 else []),
                        bank["povT"] if kt == 0 else None)
                pT_user[g % 2] = tok
                return tok

            emit_S(0)
            emit_E(0)
            if pend_tail is not None:
                pend_tail()
                pend_tail = None
            if qb == 0:
                pump_wtasks(-(-(len(wtasks) - wti[0]) // (H - h)))
            mk_dummy()
            for g in range(1, NG2):
                emit_S(g)
                emit_E(g)
                mk_dummy()
                emit_O(g - 1)
            last_ov = emit_O(NG2 - 1)

            def mk_tail(h=h, last_ov=last_ov, qb=qb):
                def f():
                    rll = b.s(lambda e: e.activation(rl_f[:], povT[32:33, :], AF.Ln),
                              last_ov)
                    rle = b.s(lambda e: e.activation(rl_bf[:], rl_f[:], AF.Exp,
                                                     scale=-1.0), rll)
                    pk = util_next()
                    pb = bmap[pk]
                    rb = b.t(lambda e, pb=pb: e.matmul(
                        pb[0:32, :], ones_row[0:1, 0:32], rl_bf[:],
                        start=True, stop=True, skip_group_check=True),
                        rle, t_ones_row, busers[pk])
                    rbe = b.v(lambda e, pb=pb: e.tensor_copy(rb_sb[:], pb[0:32, :]),
                              rb)
                    busers[pk] = rbe
                    ovn = b.v(lambda e, h=h: e.tensor_tensor(
                        ovst[h % 2][:], povT[0:32, :], rb_sb[:], ALU.mult),
                        rbe, ovst_user[h % 2])
                    bank["povT"] = ovn
                    gd = b.dma("gpsimd", ovtn[qb % 2][h // 4][(h % 4) * 32:(h % 4 + 1) * 32, :],
                               ovst[h % 2][:], ovn, ovtn_user[qb % 2][h // 4])
                    ovst_user[h % 2] = gd
                    ovn_gds.append(gd)
                    return gd
                return f

            pend_tail = mk_tail()
            if qb != 0:
                pump_fine(1)
        pend_tail()
        if qb == 0:
            while wti[0] < len(wtasks):
                pump_wtasks(8)
        else:
            pump_all()
        pending_tail = mk_qb_tail(qb, ovn_gds)



    busers["povT"] = bank["povT"]
    pairs = []
    prev_tails = None
    for j in range(NQ):
        w = ffn_work((2 * j, 2 * j + 1))
        w2 = []
        for kind, fn in w:
            if kind.startswith("slot"):
                if prev_tails is not None:
                    w2.append(("fine", prev_tails[int(kind[4])]))
            else:
                w2.append((kind, fn))
        prev_tails = mk_tails(2 * j, 2 * j + 1)
        pairs.append(w2)
    for t in prev_tails:
        pairs[-1].append(("fine", t))
    for j in range(NQ - 1):
        nxt = pairs[j + 1].pop(0)
        pairs[j].insert(len(pairs[j]) - 1, nxt)
    work = list(pending_tail)
    for pr in pairs:
        work += pr
    wk[0] = 0
    pump_all()

    gfinal = [(b.sem[k], b.c[k]) for k in
              [f"g{i}" for i in range(4)] + [f"e{i}" for i in range(4)]]

    def _final_wait(e, gfinal=gfinal):
        for smh, val in gfinal:
            e.wait_ge(smh, val)
    b.raw("gpsimd", _final_wait)

    with nc.allow_low_precision(reason="bf16 matmul operands per problem tolerance spec"):
        with nc.Block() as block:
            @block.tensor
            def _(eng):
                for f in b.q["tensor"]:
                    f(eng)

            @block.vector
            def _(eng):
                for f in b.q["vector"]:
                    f(eng)

            @block.scalar
            def _(eng):
                for f in b.q["scalar"]:
                    f(eng)

            @block.sync
            def _(eng):
                for f in b.q["sync"]:
                    f(eng)

            @block.gpsimd
            def _(eng):
                for f in b.q["gpsimd"]:
                    f(eng)

    psB.close()
    scB.close()
    stack.close()
    return nc


def kernel(**inputs):
    x = np.ascontiguousarray(np.asarray(inputs["L0_attn_h_t"], dtype=np.float32))
    B_, S, D_ = x.shape
    wmap = {
        "W_DQ": "L0_attn_W_DQ", "b_DQ": "L0_attn_b_DQ",
        "W_UQ": "L0_attn_W_UQ", "b_UQ": "L0_attn_b_UQ",
        "W_DKV": "L0_attn_W_DKV", "b_DKV": "L0_attn_b_DKV",
        "W_UK": "L0_attn_W_UK",
        "W_UV": "L0_attn_W_UV", "b_UV": "L0_attn_b_UV",
        "W_QR": "L0_attn_W_QR", "b_QR": "L0_attn_b_QR",
        "W_KR": "L0_attn_W_KR", "b_KR": "L0_attn_b_KR",
        "W_O": "L0_attn_W_O", "b_O": "L0_attn_b_O",
        "w_up": "L0_ffn_w_up", "w_down": "L0_ffn_w_down",
    }
    wd = {k: np.ascontiguousarray(np.asarray(inputs[v], dtype=np.float32))
          for k, v in wmap.items()}
    nc = build_nc(S)
    in_maps = [dict(x=x[i], **wd) for i in range(B_)]
    res = run_bass_kernel_spmd(nc, in_maps, core_ids=list(range(B_)))
    out = np.stack([np.asarray(r["out"], dtype=np.float32) for r in res.results], axis=0)
    return out
